# version 02
# baseline (speedup 1.0000x reference)
import contextlib
import numpy as np
import ml_dtypes
import concourse.bass as bass
import concourse.mybir as mybir
from concourse.bass_utils import run_bass_kernel_spmd

F32 = mybir.dt.float32
BF16 = mybir.dt.bfloat16
I32 = mybir.dt.int32
U8 = mybir.dt.uint8
AF = mybir.ActivationFunctionType
ALU = mybir.AluOpType
AX = mybir.AxisListType

BF = ml_dtypes.bfloat16

T = 2048
C = 2048
EXT = 1088
LB = 64
OWN = 1024
NH = 16
HD = 128
NCC = C // 128
H1 = 2816
NHC = 22
INNER = 8192
E = 32
LAT = 1024
EINT = 1024
SHI = 4096

QCS = [(0, 512, 12), (512, 512, 16), (1024, 64, 16)]
MIX = [[0, 1, 2, 3, 7, 8, 9, 10, 11], [3, 4, 5, 6, 7, 11, 12, 13, 14, 15], [7, 15]]
N_MIX = sum(len(m) for m in MIX)

DEBUG_TAPS = ()

NEG = -1.0e30


def _slopes():
    import math
    start = 2 ** (-2 ** (-(math.log2(NH) - 3)))
    return np.array([start * start ** i for i in range(NH)], dtype=np.float64)


def host_prep(inp):
    d = {}
    f32 = np.float32

    def t2(a):
        return np.ascontiguousarray(np.asarray(a).T)

    d["Wq"] = np.ascontiguousarray(inp["L0_attn_Wq"], f32)
    d["Wk"] = np.ascontiguousarray(inp["L0_attn_Wk"], f32)
    d["Wv"] = np.ascontiguousarray(inp["L0_attn_Wv"], f32)
    d["Wo"] = np.ascontiguousarray(inp["L0_attn_Wo"], f32)
    for nm, key in (("bq", "L0_attn_bq"), ("bk", "L0_attn_bk")):
        d[nm] = np.ascontiguousarray(np.asarray(inp[key], f32).reshape(NCC, 128).T)
    d["bv"] = np.asarray(inp["L0_attn_bv"], f32).reshape(1, C)
    d["bo"] = np.asarray(inp["L0_attn_bo"], f32).reshape(1, C)
    d["gate_w"] = t2(inp["L0_ffn_gate_w"])
    d["gate_b"] = np.asarray(inp["L0_ffn_gate_bias"], f32).reshape(1, E)
    d["fc1"] = t2(inp["L0_ffn_fc1_w"])
    d["fc2"] = t2(inp["L0_ffn_fc2_w"])
    d["w1"] = np.ascontiguousarray(np.asarray(inp["L0_ffn_w1"], f32).transpose(0, 2, 1))
    d["w2"] = np.ascontiguousarray(np.asarray(inp["L0_ffn_w2"], f32).transpose(0, 2, 1))
    d["shu"] = t2(inp["L0_ffn_shared_up_w"])
    d["shd"] = t2(inp["L0_ffn_shared_down_w"])
    d["mixk"] = np.ascontiguousarray(np.asarray(inp["L1_attn_time_mix_k"], f32).reshape(NCC, 128).T)
    d["mixr"] = np.ascontiguousarray(np.asarray(inp["L1_attn_time_mix_r"], f32).reshape(NCC, 128).T)
    d["l1Wk"] = t2(inp["L1_attn_Wk"])
    d["l1Wr"] = t2(inp["L1_attn_Wr"])
    d["l1Wv"] = t2(inp["L1_attn_Wv"])
    td = np.asarray(inp["L1_attn_time_decay"], f32)
    d["td"] = np.ascontiguousarray(td.reshape(NHC, 128).T)
    d["gam"] = np.asarray(inp["L1_attn_ln_gamma"], f32).reshape(1, C)
    d["bet"] = np.asarray(inp["L1_attn_ln_beta"], f32).reshape(1, C)
    d["f1w"] = t2(inp["L1_ffn_w1"])
    d["f1b"] = np.asarray(inp["L1_ffn_b1"], f32).reshape(1, INNER)
    d["f2w"] = t2(inp["L1_ffn_w2"])
    d["f2b"] = np.asarray(inp["L1_ffn_b2"], f32).reshape(1, C)
    return d


def percore_prep(x, core):
    b, heavy = core // 2, (core % 2 == 0)
    f32 = np.float32
    xb = np.ascontiguousarray(np.asarray(x[b], f32))
    if heavy:
        q0 = 1024
        x_ext = np.ascontiguousarray(xb[q0 - LB:q0 + OWN])
        gpos = np.arange(EXT) + (q0 - LB)
        sel_prev = np.array([[1.0, 0.0]], f32)
    else:
        q0 = 0
        x_ext = np.zeros((EXT, C), f32)
        x_ext[LB:] = xb[0:OWN]
        gpos = np.concatenate([np.full(LB, -1), np.arange(OWN)])
        sel_prev = np.array([[0.0, 1.0]], f32)

    qmax = np.where(gpos < 0, 127, gpos)

    slopes = _slopes()
    kk = np.arange(T, dtype=np.float64)
    alibi = (-slopes[:, None] * (T - 1 - kk)[None, :]).astype(f32)

    nb = sum(e for (_, _, e) in QCS)
    bias = np.zeros((NH, 128, nb), f32)
    col = 0
    for qc, (qlo, qw, ext) in enumerate(QCS):
        qmax_chunk = qmax[qlo:qlo + qw].max()
        for kt in range(ext):
            ks = np.arange(kt * 128, (kt + 1) * 128)
            b_ = alibi[:, ks].copy()
            if kt * 128 > qmax_chunk:
                b_[:] = NEG
            bias[:, :, col] = b_
            col += 1

    masks = []
    for qc, (qlo, qw, ext) in enumerate(QCS):
        for kt in MIX[qc]:
            ks = np.arange(kt * 128, (kt + 1) * 128)
            m = np.where(ks[:, None] <= qmax[None, qlo:qlo + qw], 0.0, NEG).astype(f32)
            masks.append(m.astype(BF))
    mask0 = np.ascontiguousarray(np.concatenate(
        [m for m in masks if m.shape[1] == 512], axis=1))
    mask1 = np.ascontiguousarray(np.concatenate(
        [m for m in masks if m.shape[1] == 64], axis=1))

    gate = np.ones((1, EXT), f32)
    if not heavy:
        gate[0, LB] = 0.0

    qoff = (slopes[:, None] * (T - 1 - qmax[None, :].astype(np.float64))
            ).astype(f32).astype(BF)

    bias_dev = np.ascontiguousarray(bias.transpose(1, 0, 2).reshape(128, NH * nb))
    return dict(x_full=xb, x_ext=x_ext, attn_bias=bias_dev,
                mix0=mask0, mix1=mask1, scan_gate=gate,
                qoff=np.ascontiguousarray(qoff), sel_prev=sel_prev)



class P:

    ENGS = ("sync", "gpsimd", "tensor", "vector", "scalar")

    def __init__(self, nc, es):
        self.nc = nc
        self.es = es
        self.th = {e: [] for e in self.ENGS}
        self.sem = {}
        self.cnt = {}
        self.dma_chans = set()
        self.taps = {}

    def new_sem(self, name):
        h = self.es.enter_context(self.nc.semaphore(name))
        self.sem[name] = h
        self.cnt[name] = 0
        return h

    def inc(self, name, amt):
        self.cnt[name] += amt
        return self.cnt[name]

    def val(self, name):
        return self.cnt[name]

    def on(self, eng, fn):
        self.th[eng].append(fn)

    def dma(self, qeng, out_ap, in_ap, ch):
        sem = self.sem[ch]
        self.dma_chans.add(ch)
        self.inc(ch, 16)

        def f(eng, out_ap=out_ap, in_ap=in_ap):
            eng.dma_start(out=out_ap, in_=in_ap).then_inc(sem, 16)
        self.on(qeng, f)

    def dma_tr(self, qeng, out_ap, in_ap, ch):
        sem = self.sem[ch]
        self.dma_chans.add(ch)
        self.inc(ch, 16)

        def f(eng, out_ap=out_ap, in_ap=in_ap):
            eng.dma_start(out=out_ap, in_=in_ap, transpose=True).then_inc(sem, 16)
        self.on(qeng, f)

    def wait(self, eng, ch, val=None):
        sem = self.sem[ch]
        v = self.val(ch) if val is None else val
        assert v <= self.cnt[ch], f"wait {ch}>={v} but only {self.cnt[ch]} emitted"

        def f(e, sem=sem, v=v):
            e.wait_ge(sem, v)
        self.on(eng, f)

    def run(self, eng_name, fn, ch=None, amt=1, extra=None):
        if ch is None:
            self.on(eng_name, fn)
        else:
            sem = self.sem[ch]
            self.inc(ch, amt)

            def f(e, fn=fn, sem=sem, amt=amt, extra=extra):
                i = fn(e).then_inc(sem, amt)
                if extra is not None:
                    i.then_inc(extra[0], extra[1])
            self.on(eng_name, f)

    def fence(self, eng_name, ch):
        self.wait(eng_name, ch)

    def barrier(self):
        for ch in sorted(self.dma_chans):
            self.wait("sync", ch)
        for ch in ("ten", "vec", "scl", "gps"):
            self.wait("sync", ch)
        sem = self.sem["bar"]
        tgt = self.inc("bar", len(self.ENGS))
        for e in self.ENGS:
            def f(eng, sem=sem):
                eng.nop().then_inc(sem, 1)
            self.on(e, f)
        for e in self.ENGS:
            self.wait(e, "bar", tgt)

    def claim(self, t):
        self.run("vector", lambda e, t=t: e.memset(t[:], 0), "vec")
        self.fence("vector", "vec")
        self.wait("sync", "vec")
        self.wait("gpsimd", "vec")
        self.wait("scalar", "vec")
        self.wait("tensor", "vec")

    def tap(self, name, sbuf_ap, shape, dtype):
        if name not in DEBUG_TAPS:
            return
        dt = self.nc.dram_tensor("tap_" + name, list(shape), dtype,
                                 kind="ExternalOutput").ap()
        self.dma("sync", dt, sbuf_ap, "sdma")

    def finish(self):
        nc = self.nc
        with nc.Block() as block:
            for name in self.ENGS:
                lst = self.th[name]
                def runner(eng, lst=lst):
                    for f in lst:
                        f(eng)
                getattr(block, name)(runner)


class WRing:

    def __init__(self, p, name, nslots, cols, dtype=BF16, es=None):
        self.p = p
        self.nslots = nslots
        self.bufs = [(es or p.es).enter_context(p.nc.sbuf_tensor(f"{name}{i}", [128, cols], dtype))
                     for i in range(nslots)]
        self.ld = [p.new_sem(f"{name}_ld{i}") for i in range(nslots)]
        self.ldn = [f"{name}_ld{i}" for i in range(nslots)]
        self.free_at = []
        self.n_issued = 0

    def load(self, in_ap, cols=None):
        p = self.p
        i = self.n_issued
        slot = i % self.nslots
        self.n_issued += 1
        if i >= self.nslots:
            assert len(self.free_at) >= i - self.nslots + 1, "ring: missing done()"
            for ch, v in self.free_at[i - self.nslots]:
                p.wait("gpsimd", ch, v)
        buf = self.bufs[slot]
        out_ap = buf[:] if cols is None else buf[:, 0:cols]
        p.dma("gpsimd", out_ap, in_ap, self.ldn[slot])
        return slot, buf

    def wait_granule(self, eng_name, slot_i):
        p = self.p
        slot = slot_i % self.nslots
        times = slot_i // self.nslots + 1
        p.wait(eng_name, self.ldn[slot], 16 * times)

    def done(self, n=1):
        for _ in range(n):
            self.free_at.append([("ten", self.p.val("ten"))])


def ceil_div(a, b):
    return (a + b - 1) // b


def build_program():
    nc = bass.Bass("TRN2", target_bir_lowering=False, debug=False)
    es = contextlib.ExitStack()
    p = P(nc, es)
    def din(name, shape, dtype=F32):
        return nc.dram_tensor(name, list(shape), dtype, kind="ExternalInput").ap()

    x_full = din("x_full", (T, C))
    x_ext = din("x_ext", (EXT, C))
    attn_bias = din("attn_bias", (128, NH * sum(e for (_, _, e) in QCS)))
    mix0 = din("mix0", (128, 19 * 512), BF16)
    mix1 = din("mix1", (128, 2 * 64), BF16)
    scan_gate = din("scan_gate", (1, EXT))
    qoff = din("qoff", (NH, EXT), BF16)
    sel_prev = din("sel_prev", (1, 2))
    Wq, Wk, Wv, Wo = din("Wq", (C, C)), din("Wk", (C, C)), din("Wv", (C, C)), din("Wo", (C, C))
    bq, bk = din("bq", (128, NCC)), din("bk", (128, NCC))
    bv, bo = din("bv", (1, C)), din("bo", (1, C))
    gate_w, gate_b = din("gate_w", (C, E)), din("gate_b", (1, E))
    fc1, fc2 = din("fc1", (C, LAT)), din("fc2", (LAT, C))
    w1, w2 = din("w1", (E, LAT, EINT)), din("w2", (E, EINT, LAT))
    shu, shd = din("shu", (C, SHI)), din("shd", (SHI, C))
    mixk, mixr = din("mixk", (128, NCC)), din("mixr", (128, NCC))
    l1Wk, l1Wr, l1Wv = din("l1Wk", (C, H1)), din("l1Wr", (C, H1)), din("l1Wv", (H1, C))
    td = din("td", (128, NHC))
    gam, bet = din("gam", (1, C)), din("bet", (1, C))
    f1w, f1b = din("f1w", (C, INNER)), din("f1b", (1, INNER))
    f2w, f2b = din("f2w", (INNER, C)), din("f2b", (1, C))

    out = nc.dram_tensor("out", [OWN, C], F32, kind="ExternalOutput").ap()

    def scr(name, shape, dtype=BF16):
        return nc.dram_tensor(name, list(shape), dtype, kind="Internal").ap()

    s_xnf = scr("s_xnf", (T, C))
    s_xne = scr("s_xne", (EXT, C))
    s_cat = scr("s_cat", (EXT, C))
    s_hexp = scr("s_hexp", (E, EINT, EXT))
    s_hsh = scr("s_hsh", (SHI, EXT))
    s_cw = scr("s_cw", (E, EXT), F32)
    s_ylat = scr("s_ylat", (EXT, LAT))
    s_x1 = scr("s_x1", (EXT, C), F32)
    s_xn1 = scr("s_xn1", (EXT, C))
    s_att1 = scr("s_att1", (OWN, C), F32)
    s_krp = scr("s_krp", (H1, OWN))
    s_qsc = scr("s_qsc", (EXT, 1), F32)
    s_xq1 = scr("s_xq1", (OWN, C))
    s_xq2 = scr("s_xq2", (OWN, INNER))
    s_sgn1 = scr("s_sgn1", (C, INNER))
    s_sgn2 = scr("s_sgn2", (INNER, C))

    ones_row_bf = nc.inline_tensor(np.ones((1, 128), dtype=BF), "ones_row")
    ident_f32 = nc.inline_tensor(np.eye(128, dtype=np.float32), "ident_f32")

    for s in ("sdma", "gdma", "ten", "vec", "scl", "gps", "tdma", "bar"):
        p.new_sem(s)

    sb = lambda name, shape, dtype=F32: es.enter_context(
        nc.sbuf_tensor(name, list(shape), dtype))

    NQT = EXT // 128 + (1 if EXT % 128 else 0)
    QTS = [(i * 128, min(128, EXT - i * 128)) for i in range(ceil_div(EXT, 128))]

    with es:
        c_bias = sb("c_bias", (128, NH * 44))
        c_ones = sb("c_ones", (1, 128), BF16)
        c_gateb = sb("c_gateb", (128, E))
        c_bq = sb("c_bq", (128, NCC))
        c_bk = sb("c_bk", (128, NCC))
        c_bv = sb("c_bv", (1, C), BF16)
        c_bo = sb("c_bo", (1, C), BF16)
        c_idf = sb("c_idf", (128, 128))
        cw_t = sb("cw_t", (128, ceil_div(EXT, 128) * E))
        st1 = sb("st1", (128, 1))
        st2 = sb("st2", (128, 1))
        st3 = sb("st3", (128, 1))
        st4 = sb("st4", (128, 8))
        st5 = sb("st5", (128, E))
        st6 = sb("st6", (128, E))
        st8 = sb("st8", (128, 8))
        st5g = sb("st5g", (128, E))
        qsc1 = sb("qsc1", (128, 9))
        qsc2 = sb("qsc2", (128, 8))
        c_td = sb("c_td", (128, NHC))
        c_w = sb("c_w", (128, NHC))
        c_mk = sb("c_mk", (128, NCC))
        c_mr = sb("c_mr", (128, NCC))
        c_sel = sb("c_sel", (128, 2))
        acc1 = sb("acc1", (128, 1))
        acc2 = sb("acc2", (128, 1))
        sw12 = sb("sw12", (128, 4))
        c_onesc = sb("c_onesc", (128, 1))
        c_onesr = sb("c_onesr", (1, 128))
        sc_dbg = sb("sc_dbg", (128, 9 * E)) if "scores" in DEBUG_TAPS else None
        lg_dbg = sb("lg_dbg", (128, 9 * E)) if "logits" in DEBUG_TAPS else None

        pb = [es.enter_context(nc.psum_tensor(f"pb{i}", [128, 512], F32))
              for i in range(8)]

        ring = WRing(p, "wr", 4, 4096)
        es_xnf = contextlib.ExitStack()
        xnf_fm = es_xnf.enter_context(nc.sbuf_tensor("xnf_fm", [128, NCC * T], BF16))
        es_xne = contextlib.ExitStack()
        xne_fm = es_xne.enter_context(nc.sbuf_tensor("xne_fm", [128, NCC * EXT], BF16))
        es_tmp = contextlib.ExitStack()
        tmp_a = es_tmp.enter_context(nc.sbuf_tensor("tmp_a", [128, C], F32))
        tmp_bf = es_tmp.enter_context(nc.sbuf_tensor("tmp_bf", [128, C], BF16))
        es_gate = contextlib.ExitStack()
        sbg = lambda name, shape, dtype=F32: es_gate.enter_context(
            nc.sbuf_tensor(name, list(shape), dtype))
        xne_lo = sbg("xne_lo", (128, NCC * EXT), BF16)
        es_attn = contextlib.ExitStack()

        p.dma("sync", c_bias[:], attn_bias, "sdma")
        p.dma("sync", c_ones[:], ones_row_bf.ap(), "sdma")
        p.dma("gpsimd", c_gateb[:], gate_b.broadcast_to([128, E]), "gdma")
        p.dma("sync", c_bq[:], bq, "sdma")
        p.dma("sync", c_bk[:], bk, "sdma")
        p.dma("gpsimd", c_bv[:], bv, "gdma")
        p.dma("gpsimd", c_bo[:], bo, "gdma")
        p.dma("sync", c_idf[:], ident_f32.ap(), "sdma")

        def ln0_tile(src_ap, dst_scr, t0, rows, want_lo, lo_scr):
            p.wait("sync", "vec")
            p.wait("sync", "tdma")
            p.wait("scalar", "tdma")
            p.dma("sync", tmp_a[0:rows, :], src_ap, "sdma")
            p.wait("vector", "sdma")
            p.wait("scalar", "sdma")
            p.run("vector", lambda e: e.reduce_sum(st1[0:rows], tmp_a[0:rows, :], axis=AX.X), "vec")
            p.run("scalar", lambda e: e.activation(tmp_bf[0:rows, :], tmp_a[0:rows, :], AF.Square,
                                                   accum_out=st2[0:rows]), "scl")
            p.wait("vector", "vec")
            p.wait("vector", "scl")
            p.run("vector", lambda e: e.tensor_single_scalar(st1[0:rows], st1[0:rows], 1.0 / C, ALU.mult), "vec")
            p.fence("vector", "vec")
            p.run("vector", lambda e: e.tensor_tensor(st3[0:rows], st1[0:rows], st1[0:rows], ALU.mult), "vec")
            p.run("vector", lambda e: e.tensor_scalar(st2[0:rows], st2[0:rows], 1.0 / C, 1e-5, ALU.mult, ALU.add), "vec")
            p.fence("vector", "vec")
            p.run("vector", lambda e: e.tensor_sub(st2[0:rows], st2[0:rows], st3[0:rows]), "vec")
            p.fence("vector", "vec")
            p.wait("scalar", "vec")
            p.run("scalar", lambda e: e.activation(st2[0:rows], st2[0:rows], AF.Sqrt), "scl")
            p.wait("vector", "scl")
            p.run("vector", lambda e: e.reciprocal(st2[0:rows], st2[0:rows]), "vec")
            p.fence("vector", "vec")
            p.run("vector", lambda e: e.tensor_scalar(tmp_a[0:rows, :], tmp_a[0:rows, :], st1[0:rows], None, ALU.subtract), "vec")
            p.fence("vector", "vec")
            p.run("vector", lambda e: e.tensor_scalar(tmp_a[0:rows, :], tmp_a[0:rows, :], st2[0:rows], None, ALU.mult), "vec")
            p.fence("vector", "vec")
            p.run("vector", lambda e: e.tensor_copy(tmp_bf[0:rows, :], tmp_a[0:rows, :]), "vec")
            p.fence("vector", "vec")
            p.wait("sync", "vec")
            p.dma("sync", dst_scr[t0:t0 + rows, :], tmp_bf[0:rows, :], "tdma")
            if want_lo:
                p.run("vector", lambda e: e.tensor_sub(tmp_a[0:rows, :], tmp_a[0:rows, :], tmp_bf[0:rows, :]), "vec")
                p.fence("vector", "vec")
                p.wait("vector", "tdma")
                p.run("vector", lambda e: e.tensor_copy(tmp_bf[0:rows, :], tmp_a[0:rows, :]), "vec")
                p.fence("vector", "vec")
                p.wait("sync", "vec")
                p.dma("sync", lo_scr[t0:t0 + rows, :], tmp_bf[0:rows, :], "tdma")

        s_xne_lo = scr("s_xne_lo", (EXT, C))
        for i in range(T // 128):
            ln0_tile(x_full[i * 128:(i + 1) * 128, :], s_xnf, i * 128, 128, False, None)
        for (t0, rows) in QTS:
            ln0_tile(x_ext[t0:t0 + rows, :], s_xne, t0, rows, True, s_xne_lo)

        p.wait("sync", "tdma")
        for cc in range(NCC):
            p.dma_tr("sync", xnf_fm[:, cc * T:(cc + 1) * T],
                     s_xnf[:, cc * 128:(cc + 1) * 128], "tdma")
            p.dma_tr("sync", xne_fm[:, cc * EXT:(cc + 1) * EXT],
                     s_xne[:, cc * 128:(cc + 1) * 128], "tdma")
            p.dma_tr("sync", xne_lo[:, cc * EXT:(cc + 1) * EXT],
                     s_xne_lo[:, cc * 128:(cc + 1) * 128], "tdma")

        gwf = sbg("gwf", (128, NCC * E))
        gw_hi = sbg("gw_hi", (128, NCC * E), BF16)
        gw_lo = sbg("gw_lo", (128, NCC * E), BF16)
        p.dma("sync", gwf[:].rearrange("p (cc e) -> p cc e", cc=NCC),
              gate_w.rearrange("(cc p) e -> p cc e", p=128), "sdma")
        p.wait("vector", "sdma")
        p.run("vector", lambda e: e.tensor_copy(gw_hi[:], gwf[:]), "vec")
        p.fence("vector", "vec")
        p.run("vector", lambda e: e.tensor_sub(gwf[:], gwf[:], gw_hi[:]), "vec")
        p.fence("vector", "vec")
        p.run("vector", lambda e: e.tensor_copy(gw_lo[:], gwf[:]), "vec")
        p.fence("vector", "vec")

        p.wait("tensor", "tdma")
        p.wait("tensor", "vec")
        for ti, (t0, rows) in enumerate(QTS):
            ps = pb[ti % 2]
            def mmgate(e, ti=ti, t0=t0, rows=rows, ps=ps):
                first = True
                last = None
                for cc in range(NCC):
                    for (xa, wa) in ((xne_fm, gw_hi), (xne_fm, gw_lo), (xne_lo, gw_hi)):
                        last = e.matmul(ps[0:rows, 0:E],
                                 xa[:, cc * EXT + t0: cc * EXT + t0 + rows],
                                 wa[:, cc * E:(cc + 1) * E],
                                 start=first, stop=(cc == NCC - 1 and wa is gw_hi and xa is xne_lo))
                        first = False
                return last
            if ti >= 2:
                p.wait("tensor", "scl")
            p.run("tensor", mmgate, "ten")
            p.wait("scalar", "ten")
            p.wait("scalar", "vec")
            if "logits" in DEBUG_TAPS:
                p.wait("vector", "ten")
                p.run("vector", lambda e, ti=ti, rows=rows, ps=ps: e.tensor_copy(
                    lg_dbg[0:rows, ti * E:(ti + 1) * E], ps[0:rows, 0:E]), "vec")
                p.fence("vector", "vec")
                p.wait("scalar", "vec")
            p.run("scalar", lambda e, ps=ps, rows=rows: e.activation(
                st5[0:rows], ps[0:rows, 0:E], AF.Sigmoid), "scl")
            p.wait("vector", "scl")
            p.wait("vector", "gdma")
            if "scores" in DEBUG_TAPS:
                p.run("vector", lambda e, ti=ti, rows=rows: e.tensor_copy(
                    sc_dbg[0:rows, ti * E:(ti + 1) * E], st5[0:rows]), "vec")
                p.fence("vector", "vec")
            p.run("vector", lambda e, rows=rows: e.tensor_add(st6[0:rows], st5[0:rows], c_gateb[0:rows]), "vec")
            p.fence("vector", "vec")
            for g in range(4):
                p.run("vector", lambda e, g=g, rows=rows: e.max(st8[0:rows], st6[0:rows, g * 8:(g + 1) * 8]), "vec")
                p.fence("vector", "vec")
                p.run("vector", lambda e, g=g, rows=rows: e.tensor_add(
                    st5g[0:rows, g * 8:(g + 1) * 8],
                    st8[0:rows, 0:1].broadcast_to([rows, 8]),
                    st8[0:rows, 1:2].broadcast_to([rows, 8])), "vec")
                p.fence("vector", "vec")
            p.run("vector", lambda e, rows=rows: e.memset(st4[0:rows, 4:8], -64.0), "vec")
            p.fence("vector", "vec")
            p.run("vector", lambda e, rows=rows: e.tensor_copy(
                st4[0:rows, 0:4], st5g[0:rows, 0:32:8]), "vec")
            p.fence("vector", "vec")
            p.run("vector", lambda e, rows=rows: e.max(st8[0:rows], st4[0:rows, 0:8]), "vec")
            p.fence("vector", "vec")
            p.run("vector", lambda e, rows=rows: e.tensor_scalar(
                st5g[0:rows], st5g[0:rows], st8[0:rows, 1:2], None, ALU.is_ge), "vec")
            p.fence("vector", "vec")
            p.run("vector", lambda e, rows=rows: e.scalar_tensor_tensor(
                st6[0:rows], st6[0:rows], 64.0, st5g[0:rows], ALU.add, ALU.mult), "vec")
            p.fence("vector", "vec")
            p.run("vector", lambda e, rows=rows: e.max(st8[0:rows], st6[0:rows]), "vec")
            p.fence("vector", "vec")
            p.run("vector", lambda e, rows=rows: e.tensor_scalar(
                st6[0:rows], st6[0:rows], st8[0:rows, 7:8], None, ALU.is_ge), "vec")
            p.fence("vector", "vec")
            p.run("vector", lambda e, rows=rows: e.tensor_mul(st6[0:rows], st6[0:rows], st5[0:rows]), "vec")
            p.fence("vector", "vec")
            p.run("vector", lambda e, rows=rows: e.reduce_sum(st1[0:rows], st6[0:rows], axis=AX.X), "vec")
            p.fence("vector", "vec")
            p.run("vector", lambda e, rows=rows: e.reciprocal(st2[0:rows], st1[0:rows]), "vec")
            p.fence("vector", "vec")
            p.run("vector", lambda e, rows=rows: e.tensor_scalar(
                st2[0:rows], st2[0:rows], 2.5, None, ALU.mult), "vec")
            p.fence("vector", "vec")
            p.run("vector", lambda e, ti=ti, rows=rows: e.tensor_scalar(
                cw_t[0:rows, ti * E:(ti + 1) * E], st6[0:rows], st2[0:rows], None, ALU.mult), "vec")
            p.fence("vector", "vec")
        p.barrier()
        es_gate.close()
        es_tmp.close()
        cstg = [es.enter_context(nc.sbuf_tensor(f"cstg{i}", [128, 128], BF16, side="right"))
                for i in range(2)]
        c_qoff = es.enter_context(nc.sbuf_tensor("c_qoff", [1, EXT], BF16, side="right"))
        q_all = es_attn.enter_context(nc.sbuf_tensor("q_all", [128, NH * EXT], BF16, side="right"))
        p.claim(q_all)
        RS = 0.08838834764831845
        p.wait("vector", "sdma")
        p.run("vector", lambda e: e.tensor_single_scalar(c_bq[:], c_bq[:], RS, ALU.mult), "vec")
        p.fence("vector", "vec")
        p.wait("scalar", "vec")
        EXT_SEGS = [(0, 512), (512, 512), (1024, 64)]
        for h in range(NH):
            gi = ring.n_issued
            ring.load(Wq.rearrange("(cc p) o -> p cc o", p=128)[:, :, h * 128:(h + 1) * 128],
                      cols=2048)
            grp = 0 if (h % 2 == 0) else 3
            if h >= 2:
                p.wait("tensor", "scl")
            ring.wait_granule("tensor", gi)

            def mmq(e, h=h, grp=grp, gbuf=ring.bufs[gi % ring.nslots]):
                last = None
                for cc in range(NCC):
                    for si, (s0, swd) in enumerate(EXT_SEGS):
                        last = e.matmul(pb[grp + si][0:128, 0:swd],
                                        gbuf[:, cc * 128:(cc + 1) * 128],
                                        xne_fm[:, cc * EXT + s0: cc * EXT + s0 + swd],
                                        start=(cc == 0), stop=(cc == NCC - 1))
                return last
            p.run("tensor", mmq, "ten")
            ring.done(1)
            p.wait("scalar", "ten")
            for si, (s0, swd) in enumerate(EXT_SEGS):
                p.run("scalar", lambda e, h=h, grp=grp, si=si, s0=s0, swd=swd: e.activation(
                    q_all[:, h * EXT + s0: h * EXT + s0 + swd],
                    pb[grp + si][0:128, 0:swd], AF.Identity,
                    bias=c_bq[:, h:h + 1], scale=RS), "scl")
        p.barrier()
        es_xne.close()
        c_mix0 = es_attn.enter_context(nc.sbuf_tensor("c_mix0", [128, 19 * 512], BF16, side="right"))
        p.claim(c_mix0)
        c_mix1 = es_attn.enter_context(nc.sbuf_tensor("c_mix1", [128, 2 * 64], BF16, side="right"))
        p.claim(c_mix1)
        p.dma("sync", c_mix0[:], mix0, "sdma")
        p.dma("sync", c_mix1[:], mix1, "sdma")
        p.wait("vector", "sdma")

        v4 = es_attn.enter_context(nc.sbuf_tensor("v4", [128, 16 * 516], BF16, side="right"))
        p.claim(v4)
        k_fm = es_attn.enter_context(nc.sbuf_tensor("k_fm", [128, T], BF16, side="right"))
        p.claim(k_fm)
        pT = es_attn.enter_context(nc.sbuf_tensor("pT", [128, 16 * 512], BF16, side="right"))
        p.claim(pT)
        for i in range(2):
            p.new_sem(f"catd{i}")
        n_cat = [0]
        QCOFF = [0, 12, 28]
        mix0_idx = {}
        mi = 0
        for qc in range(2):
            for kt in MIX[qc]:
                mix0_idx[(qc, kt)] = mi
                mi += 1
        mix1_idx = {(2, kt): i for i, kt in enumerate(MIX[2])}

        p.run("vector", lambda e: e.memset(
            v4[:].rearrange("p (m c) -> p m c", c=129)[:, :, 128:129], 1.0), "vec")
        p.fence("vector", "vec")

        for hg in range(4):
            gis = []
            for half in range(2):
                gis.append(ring.n_issued)
                ring.load(Wv.rearrange("(cc p) o -> p cc o", p=128)
                          [:, half * 8:(half + 1) * 8, hg * 512:(hg + 1) * 512])
            p.wait("tensor", "vec")
            for g in gis:
                ring.wait_granule("tensor", g)
            gb0 = ring.bufs[gis[0] % ring.nslots]
            gb1 = ring.bufs[gis[1] % ring.nslots]
            for tt in range(16):
                psv = pb[6 + (tt % 2)]
                if tt >= 2:
                    p.wait("tensor", "vec")

                def mmv(e, tt=tt, psv=psv, gb0=gb0, gb1=gb1, hg=hg):
                    for cc in range(NCC):
                        gb = gb0 if cc < 8 else gb1
                        e.matmul(psv[0:128, 0:512],
                                 xnf_fm[:, cc * T + tt * 128: cc * T + tt * 128 + 128],
                                 gb[:, (cc % 8) * 512:(cc % 8) * 512 + 512],
                                 start=(cc == 0), stop=False)
                    return e.matmul(psv[0:128, 0:512], c_ones[0:1, :],
                                    c_bv[0:1, hg * 512:(hg + 1) * 512],
                                    start=False, stop=True)
                p.run("tensor", mmv, "ten")
                if tt == 15:
                    ring.done(2)
                p.wait("vector", "ten")
                p.run("vector", lambda e, tt=tt, psv=psv: e.tensor_copy(
                    v4[:, tt * 516: tt * 516 + 516].rearrange(
                        "p (h c) -> p h c", h=4)[:, :, 0:128],
                    psv[0:128, 0:512].rearrange("p (h c) -> p h c", h=4)), "vec")
                p.fence("vector", "vec")
            for hh in range(4):
                h = hg * 4 + hh
                gi = ring.n_issued
                ring.load(Wk.rearrange("(cc p) o -> p cc o", p=128)[:, :, h * 128:(h + 1) * 128],
                          cols=2048)
                p.wait("tensor", "scl")
                p.wait("tensor", "vec")
                ring.wait_granule("tensor", gi)

                def mmk(e, gbuf=ring.bufs[gi % ring.nslots]):
                    last = None
                    for cc in range(NCC):
                        for seg in range(4):
                            last = e.matmul(pb[2 + seg][0:128, 0:512],
                                            gbuf[:, cc * 128:(cc + 1) * 128],
                                            xnf_fm[:, cc * T + seg * 512: cc * T + seg * 512 + 512],
                                            start=(cc == 0), stop=(cc == NCC - 1))
                    return last
                p.run("tensor", mmk, "ten")
                ring.done(1)
                p.wait("scalar", "ten")
                for seg in range(4):
                    p.run("scalar", lambda e, seg=seg, h=h: e.activation(
                        k_fm[:, seg * 512:(seg + 1) * 512], pb[2 + seg][0:128, 0:512],
                        AF.Identity, bias=c_bk[:, h:h + 1]), "scl")

                p.wait("sync", "ten")
                p.dma("sync", c_qoff[:], qoff[h:h + 1, :], "sdma")
                p.wait("tensor", "sdma")
                for qc, (qlo, qw, extn) in enumerate(QCS):
                    for kt in range(extn):
                        ps_s = pb[6 + (kt % 2)]
                        p.wait("tensor", "scl")
                        p.wait("tensor", "vec")

                        def mms(e, ps_s=ps_s, kt=kt, qlo=qlo, qw=qw, h=h):
                            e.matmul(ps_s[0:128, 0:qw], k_fm[:, kt * 128:(kt + 1) * 128],
                                     q_all[:, h * EXT + qlo: h * EXT + qlo + qw],
                                     start=True, stop=False)
                            return e.matmul(ps_s[0:128, 0:qw], c_ones[0:1, :],
                                            c_qoff[0:1, qlo:qlo + qw],
                                            start=False, stop=True)
                        p.run("tensor", mms, "ten")
                        in_mix = kt in MIX[qc]
                        if in_mix:
                            if qc < 2:
                                mi = mix0_idx[(qc, kt)]
                                msrc, moff = c_mix0, mi * 512
                            else:
                                mi = mix1_idx[(2, kt)]
                                msrc, moff = c_mix1, mi * 64
                            p.wait("vector", "ten")
                            p.run("vector", lambda e, ps_s=ps_s, qw=qw, msrc=msrc, moff=moff: e.tensor_add(
                                ps_s[0:128, 0:qw], ps_s[0:128, 0:qw],
                                msrc[:, moff: moff + qw]), "vec")
                            p.wait("scalar", "vec")
                        p.wait("scalar", "ten")
                        bcol = h * 44 + QCOFF[qc] + kt
                        p.run("scalar", lambda e, ps_s=ps_s, kt=kt, qw=qw, bcol=bcol: e.activation(
                            pT[:, kt * 512: kt * 512 + qw], ps_s[0:128, 0:qw],
                            AF.Exp, bias=c_bias[:, bcol:bcol + 1]), "scl")
                    p.wait("tensor", "scl")
                    p.wait("tensor", "vec")
                    nsub = ceil_div(qw, 128)
                    for sub in range(nsub):
                        sw = min(128, qw - sub * 128)
                        ps_o = pb[2 + (sub % 2)]
                        p.wait("tensor", "vec")

                        def mmpv(e, extn=extn, kt0=0, sub=sub, sw=sw, ps_o=ps_o, hh=hh):
                            last = None
                            for kt in range(extn):
                                last = e.matmul(
                                    ps_o[0:sw, 0:129],
                                    pT[:, kt * 512 + sub * 128: kt * 512 + sub * 128 + sw],
                                    v4[:, kt * 516 + hh * 129: kt * 516 + hh * 129 + 129],
                                    start=(kt == 0), stop=(kt == extn - 1))
                            return last
                        p.run("tensor", mmpv, "ten")
                        p.wait("vector", "ten")
                        p.run("vector", lambda e, ps_o=ps_o, sw=sw: e.reciprocal(
                            st1[0:sw], ps_o[0:sw, 128:129]), "vec")
                        p.fence("vector", "vec")
                        cslot = n_cat[0] % 2
                        if n_cat[0] >= 2:
                            p.wait("vector", f"catd{cslot}",
                                   16 * (n_cat[0] // 2))
                        p.run("vector", lambda e, ps_o=ps_o, sw=sw, cslot=cslot: e.tensor_scalar(
                            cstg[cslot][0:sw, :], ps_o[0:sw, 0:128], st1[0:sw], None,
                            ALU.mult), "vec")
                        p.fence("vector", "vec")
                        p.wait("sync", "vec")
                        r0 = qlo + sub * 128
                        p.dma("sync", s_cat[r0:r0 + sw, h * 128:(h + 1) * 128],
                              cstg[cslot][0:sw, :], f"catd{cslot}")
                        n_cat[0] += 1

        p.barrier()
        es_attn.close()
        es_xnf.close()

        es_p3 = contextlib.ExitStack()
        cat_fm = es_p3.enter_context(nc.sbuf_tensor("cat_fm", [128, NCC * EXT], BF16, side="right"))
        p.claim(cat_fm)
        es_x1 = contextlib.ExitStack()
        x1_acc = es_x1.enter_context(nc.sbuf_tensor("x1_acc", [128, len(QTS) * C], F32))
        for cc in range(NCC):
            p.dma_tr("sync", cat_fm[:, cc * EXT:(cc + 1) * EXT],
                     s_cat[:, cc * 128:(cc + 1) * 128], "tdma")
        p.run("vector", lambda e: e.memset(x1_acc[:], 0.0), "vec")
        p.fence("vector", "vec")
        p.wait("sync", "vec")
        for ti, (t0, rows) in enumerate(QTS):
            p.dma("sync", x1_acc[0:rows, ti * C:(ti + 1) * C], x_ext[t0:t0 + rows, :], "sdma")
        p.wait("tensor", "tdma")
        for ob in range(4):
            gis = []
            for half in range(2):
                gis.append(ring.n_issued)
                ring.load(Wo.rearrange("(cc p) o -> p cc o", p=128)
                          [:, half * 8:(half + 1) * 8, ob * 512:(ob + 1) * 512])
            for g in gis:
                ring.wait_granule("tensor", g)
            gb0 = ring.bufs[gis[0] % ring.nslots]
            gb1 = ring.bufs[gis[1] % ring.nslots]
            for ti, (t0, rows) in enumerate(QTS):
                pso = pb[ti % 2]
                p.wait("tensor", "vec")

                def mmo(e, t0=t0, rows=rows, pso=pso, ob=ob, gb0=gb0, gb1=gb1):
                    for cc in range(NCC):
                        gb = gb0 if cc < 8 else gb1
                        e.matmul(pso[0:rows, 0:512],
                                 cat_fm[:, cc * EXT + t0: cc * EXT + t0 + rows],
                                 gb[:, (cc % 8) * 512:(cc % 8) * 512 + 512],
                                 start=(cc == 0), stop=False)
                    return e.matmul(pso[0:rows, 0:512], c_ones[0:1, 0:rows],
                                    c_bo[0:1, ob * 512:(ob + 1) * 512],
                                    start=False, stop=True)
                p.run("tensor", mmo, "ten")
                if ti == len(QTS) - 1:
                    ring.done(2)
                p.wait("vector", "ten")
                p.wait("vector", "sdma")
                p.run("vector", lambda e, ti=ti, rows=rows, ob=ob, pso=pso: e.tensor_add(
                    x1_acc[0:rows, ti * C + ob * 512: ti * C + ob * 512 + 512],
                    x1_acc[0:rows, ti * C + ob * 512: ti * C + ob * 512 + 512],
                    pso[0:rows, 0:512]), "vec")
                p.fence("vector", "vec")
        p.barrier()
        es_p3.close()
        p.tap("x1a", x1_acc[:], (128, len(QTS) * C), F32)

        hstg = [es.enter_context(nc.sbuf_tensor(f"hstg{i}", [128, EXT], BF16, side="right"))
                for i in range(2)]
        hblk = es.enter_context(nc.sbuf_tensor("hblk", [128, 2 * 4096], BF16, side="right"))
        es_p4x = contextlib.ExitStack()
        x_lat = es_p4x.enter_context(nc.sbuf_tensor("x_lat", [128, 8 * EXT], BF16, side="right"))
        p.claim(x_lat)
        es_p4a = contextlib.ExitStack()
        xne2 = es_p4a.enter_context(nc.sbuf_tensor("xne2", [128, NCC * EXT], BF16, side="right"))
        p.claim(xne2)
        for cc in range(NCC):
            p.dma_tr("sync", xne2[:, cc * EXT:(cc + 1) * EXT],
                     s_xne[:, cc * 128:(cc + 1) * 128], "tdma")
        p.wait("tensor", "tdma")
        for lg in range(4):
            gi = ring.n_issued
            ring.load(fc1.rearrange("(cc p) o -> p cc o", p=128)
                      [:, :, lg * 256:(lg + 1) * 256], cols=4096)
            ring.wait_granule("tensor", gi)
            gbuf = ring.bufs[gi % ring.nslots]
            for lt in range(2):
                li = lg * 2 + lt
                grp = 0 if li % 2 == 0 else 3
                p.wait("tensor", "scl")

                def mmf1(e, lt=lt, grp=grp, gbuf=gbuf):
                    last = None
                    for cc in range(NCC):
                        for si, (s0, swd) in enumerate(EXT_SEGS):
                            last = e.matmul(pb[grp + si][0:128, 0:swd],
                                            gbuf[:, cc * 256 + lt * 128: cc * 256 + lt * 128 + 128],
                                            xne2[:, cc * EXT + s0: cc * EXT + s0 + swd],
                                            start=(cc == 0), stop=(cc == NCC - 1))
                    return last
                p.run("tensor", mmf1, "ten")
                p.wait("scalar", "ten")
                for si, (s0, swd) in enumerate(EXT_SEGS):
                    p.run("scalar", lambda e, li=li, grp=grp, si=si, s0=s0, swd=swd: e.activation(
                        x_lat[:, li * EXT + s0: li * EXT + s0 + swd],
                        pb[grp + si][0:128, 0:swd], AF.Copy), "scl")
            ring.done(1)

        for i in range(2):
            p.new_sem(f"hsd{i}")
        n_hst = [0]

        def spill_h_tile(src_fn, dst_ap, relu2=True, cwslot=None):
            pass

        for ig in range(16):
            gi = ring.n_issued
            ring.load(shu.rearrange("(cc p) o -> p cc o", p=128)
                      [:, :, ig * 256:(ig + 1) * 256], cols=4096)
            ring.wait_granule("tensor", gi)
            gbuf = ring.bufs[gi % ring.nslots]
            for it2 in range(2):
                it = ig * 2 + it2
                grp = 0 if it % 2 == 0 else 3
                p.wait("tensor", "scl")

                def mmsu(e, it2=it2, grp=grp, gbuf=gbuf):
                    last = None
                    for cc in range(NCC):
                        for si, (s0, swd) in enumerate(EXT_SEGS):
                            last = e.matmul(pb[grp + si][0:128, 0:swd],
                                            gbuf[:, cc * 256 + it2 * 128: cc * 256 + it2 * 128 + 128],
                                            xne2[:, cc * EXT + s0: cc * EXT + s0 + swd],
                                            start=(cc == 0), stop=(cc == NCC - 1))
                    return last
                p.run("tensor", mmsu, "ten")
                slot = n_hst[0] % 2
                if n_hst[0] >= 2:
                    p.wait("scalar", f"hsd{slot}", 16 * (n_hst[0] // 2))
                    p.wait("vector", f"hsd{slot}", 16 * (n_hst[0] // 2))
                p.wait("scalar", "ten")
                for si, (s0, swd) in enumerate(EXT_SEGS):
                    p.run("scalar", lambda e, grp=grp, si=si, s0=s0, swd=swd, slot=slot: e.activation(
                        hstg[slot][:, s0:s0 + swd], pb[grp + si][0:128, 0:swd],
                        AF.Relu), "scl")
                p.wait("vector", "scl")
                p.run("vector", lambda e, slot=slot: e.tensor_mul(
                    hstg[slot][:], hstg[slot][:], hstg[slot][:]), "vec")
                p.fence("vector", "vec")
                p.wait("sync", "vec")
                p.dma("sync", s_hsh[it * 128:(it + 1) * 128, :].rearrange("i t -> i t"),
                      hstg[slot][:], f"hsd{slot}")
                n_hst[0] += 1
            ring.done(1)
        p.barrier()
        es_p4a.close()

        p.new_sem("hbd")
        TPAIRS = [(0, 2), (2, 2), (4, 2), (6, 2), (8, 1)]
        for tpi, (tp0, ntt) in enumerate(TPAIRS):
            tw = sum(QTS[tp0 + j][1] for j in range(ntt))
            t0 = QTS[tp0][0]
            p.wait("sync", "ten")
            p.dma("sync", hblk[:].rearrange("p (ic t) -> p ic t", ic=32)[:, :, 0:tw],
                  s_hsh.rearrange("(ic p) t -> p ic t", p=128)[:, :, t0:t0 + tw], "hbd")
            p.wait("tensor", "hbd")
            for ob in range(4):
                gi = ring.n_issued
                ring.load(shd.rearrange("(ic p) o -> p ic o", p=128)
                          [:, 0:8, ob * 512:(ob + 1) * 512], cols=4096)
                gi2 = ring.n_issued
                ring.load(shd.rearrange("(ic p) o -> p ic o", p=128)
                          [:, 8:16, ob * 512:(ob + 1) * 512], cols=4096)
                gi3 = ring.n_issued
                ring.load(shd.rearrange("(ic p) o -> p ic o", p=128)
                          [:, 16:24, ob * 512:(ob + 1) * 512], cols=4096)
                gi4 = ring.n_issued
                ring.load(shd.rearrange("(ic p) o -> p ic o", p=128)
                          [:, 24:32, ob * 512:(ob + 1) * 512], cols=4096)
                for g in (gi, gi2, gi3, gi4):
                    ring.wait_granule("tensor", g)
                gbs = [ring.bufs[g % ring.nslots] for g in (gi, gi2, gi3, gi4)]
                for j in range(ntt):
                    ti = tp0 + j
                    rows = QTS[ti][1]
                    pso = pb[(ob % 2) * 4 + j * 2]
                    pso = pb[2 * j + (ob % 2) * 4]

                    def mmsd(e, j=j, rows=rows, pso=pso, gbs=gbs, tw=tw):
                        last = None
                        for ic in range(32):
                            gb = gbs[ic // 8]
                            last = e.matmul(
                                pso[0:rows, 0:512],
                                hblk[:].rearrange("p (ic t) -> p ic t", ic=32)
                                [:, ic, j * 128: j * 128 + rows],
                                gb[:, (ic % 8) * 512:(ic % 8) * 512 + 512],
                                start=(ic == 0), stop=(ic == 31))
                        return last
                    p.wait("tensor", "vec")
                    p.run("tensor", mmsd, "ten")
                    p.wait("vector", "ten")
                    p.run("vector", lambda e, ti=ti, rows=rows, ob=ob, pso=pso: e.tensor_add(
                        x1_acc[0:rows, ti * C + ob * 512: ti * C + ob * 512 + 512],
                        x1_acc[0:rows, ti * C + ob * 512: ti * C + ob * 512 + 512],
                        pso[0:rows, 0:512]), "vec")
                    p.fence("vector", "vec")
                ring.done(4)
        p.barrier()

        es_p4c = contextlib.ExitStack()
        cwTs = es_p4c.enter_context(nc.sbuf_tensor("cwTs", [E, EXT], F32, side="right"))
        p.claim(cwTs)
        p.wait("tensor", "sdma")
        p.wait("tensor", "vec")
        for ti, (t0, rows) in enumerate(QTS):
            p.wait("tensor", "vec")
            p.run("tensor", lambda e, ti=ti, rows=rows: e.transpose(
                pb[ti % 2][0:E, 0:rows].rearrange("e r -> e r"),
                cw_t[0:rows, ti * E:(ti + 1) * E], c_idf[0:rows, 0:rows]), "ten")
            p.wait("vector", "ten")
            p.run("vector", lambda e, ti=ti, t0=t0, rows=rows: e.tensor_copy(
                cwTs[:, t0:t0 + rows], pb[ti % 2][0:E, 0:rows]), "vec")
            p.fence("vector", "vec")
        p.wait("sync", "vec")
        p.dma("sync", s_cw, cwTs[:], "sdma")
        p.barrier()

        cwb = [es_p4c.enter_context(nc.sbuf_tensor(f"cwb{i}", [128, EXT], BF16, side="right"))
               for i in range(2)]
        for _t in cwb:
            p.claim(_t)
        p.new_sem("cwd0")
        p.new_sem("cwd1")
        for e_i in range(E):
            cslot = e_i % 2
            if e_i >= 2:
                p.wait("gpsimd", "vec")
            p.dma("gpsimd", cwb[cslot][:], s_cw[e_i:e_i + 1, :].broadcast_to([128, EXT]),
                  f"cwd{cslot}")
            for half in range(2):
                gi = ring.n_issued
                ring.load(w1.rearrange("e (lc p) i -> e p lc i", p=128)
                          [e_i][:, :, half * 512:(half + 1) * 512], cols=4096)
                ring.wait_granule("tensor", gi)
                gbuf = ring.bufs[gi % ring.nslots]
                for it2 in range(4):
                    it = half * 4 + it2
                    grp = 0 if it % 2 == 0 else 3
                    p.wait("tensor", "scl")

                    def mme1(e, it2=it2, grp=grp, gbuf=gbuf):
                        last = None
                        for lc in range(8):
                            for si, (s0, swd) in enumerate(EXT_SEGS):
                                last = e.matmul(pb[grp + si][0:128, 0:swd],
                                                gbuf[:, lc * 512 + it2 * 128: lc * 512 + it2 * 128 + 128],
                                                x_lat[:, lc * EXT + s0: lc * EXT + s0 + swd],
                                                start=(lc == 0), stop=(lc == 7))
                        return last
                    p.run("tensor", mme1, "ten")
                    slot = n_hst[0] % 2
                    if n_hst[0] >= 2:
                        p.wait("scalar", f"hsd{slot}", 16 * (n_hst[0] // 2))
                        p.wait("vector", f"hsd{slot}", 16 * (n_hst[0] // 2))
                    p.wait("scalar", "ten")
                    for si, (s0, swd) in enumerate(EXT_SEGS):
                        p.run("scalar", lambda e, grp=grp, si=si, s0=s0, swd=swd, slot=slot: e.activation(
                            hstg[slot][:, s0:s0 + swd], pb[grp + si][0:128, 0:swd],
                            AF.Relu), "scl")
                    p.wait("vector", "scl")
                    p.wait("vector", f"cwd{cslot}", 16 * (e_i // 2 + 1))
                    p.run("vector", lambda e, slot=slot: e.tensor_mul(
                        hstg[slot][:], hstg[slot][:], hstg[slot][:]), "vec")
                    p.fence("vector", "vec")
                    p.run("vector", lambda e, slot=slot, cslot=cslot: e.tensor_mul(
                        hstg[slot][:], hstg[slot][:], cwb[cslot][:]), "vec")
                    p.fence("vector", "vec")
                    p.wait("sync", "vec")
                    p.dma("sync", s_hexp[e_i, it * 128:(it + 1) * 128, :],
                          hstg[slot][:], f"hsd{slot}")
                    n_hst[0] += 1
                ring.done(1)
        p.barrier()
        es_p4c.close()
        es_p4x.close()

        es_p4y = contextlib.ExitStack()
        ylat_bf = es_p4y.enter_context(nc.sbuf_tensor("ylat_bf", [128, len(QTS) * LAT], BF16, side="right"))
        p.claim(ylat_bf)
        EPASS = [(0, 4, 512), (4, 4, 512), (8, 1, 64)]
        for (tp0, ntt, tw) in EPASS:
            t0 = QTS[tp0][0]
            for e_i in range(E):
                gi = ring.n_issued
                ring.load(s_hexp.rearrange("e (ic p) t -> e p ic t", p=128)
                          [e_i][:, :, t0:t0 + tw], cols=8 * tw)
                gw0 = ring.n_issued
                ring.load(w2.rearrange("e (ic p) l -> e p ic l", p=128)
                          [e_i][:, :, 0:512], cols=4096)
                gw1 = ring.n_issued
                ring.load(w2.rearrange("e (ic p) l -> e p ic l", p=128)
                          [e_i][:, :, 512:1024], cols=4096)
                for g in (gi, gw0, gw1):
                    ring.wait_granule("tensor", g)
                hb = ring.bufs[gi % ring.nslots]
                wb = [ring.bufs[gw0 % ring.nslots], ring.bufs[gw1 % ring.nslots]]

                def mme2(e, e_i=e_i, ntt=ntt, tw=tw, tp0=tp0, hb=hb, wb=wb):
                    last = None
                    for j in range(ntt):
                        rows = QTS[tp0 + j][1]
                        for ic in range(8):
                            for lh in range(2):
                                last = e.matmul(
                                    pb[2 * j + lh][0:rows, 0:512],
                                    hb[:, ic * tw + j * 128: ic * tw + j * 128 + rows],
                                    wb[lh][:, ic * 512:(ic + 1) * 512],
                                    start=(ic == 0 and e_i == 0),
                                    stop=(ic == 7 and e_i == E - 1))
                    return last
                p.run("tensor", mme2, "ten")
                ring.done(3)
            p.wait("scalar", "ten")
            for j in range(ntt):
                ti = tp0 + j
                rows = QTS[ti][1]
                for lh in range(2):
                    p.run("scalar", lambda e, ti=ti, j=j, lh=lh, rows=rows: e.activation(
                        ylat_bf[0:rows, ti * LAT + lh * 512: ti * LAT + lh * 512 + 512],
                        pb[2 * j + lh][0:rows, 0:512], AF.Copy), "scl")
            p.wait("tensor", "scl")
        p.wait("sync", "scl")
        for ti, (t0, rows) in enumerate(QTS):
            p.dma("sync", s_ylat[t0:t0 + rows, :], ylat_bf[0:rows, ti * LAT:(ti + 1) * LAT], "tdma")
        p.barrier()
        ylat_fm = es_p4y.enter_context(nc.sbuf_tensor("ylat_fm", [128, 8 * EXT], BF16, side="right"))
        p.claim(ylat_fm)
        for lc in range(8):
            p.dma_tr("sync", ylat_fm[:, lc * EXT:(lc + 1) * EXT],
                     s_ylat[:, lc * 128:(lc + 1) * 128], "tdma")
        p.wait("tensor", "tdma")
        for ob in range(4):
            gi = ring.n_issued
            ring.load(fc2.rearrange("(lc p) o -> p lc o", p=128)
                      [:, :, ob * 512:(ob + 1) * 512], cols=4096)
            ring.wait_granule("tensor", gi)
            gbuf = ring.bufs[gi % ring.nslots]
            for ti, (t0, rows) in enumerate(QTS):
                pso = pb[ti % 2]
                p.wait("tensor", "vec")

                def mmf2(e, t0=t0, rows=rows, pso=pso, gbuf=gbuf):
                    last = None
                    for lc in range(8):
                        last = e.matmul(pso[0:rows, 0:512],
                                        ylat_fm[:, lc * EXT + t0: lc * EXT + t0 + rows],
                                        gbuf[:, lc * 512:(lc + 1) * 512],
                                        start=(lc == 0), stop=(lc == 7))
                    return last
                p.run("tensor", mmf2, "ten")
                if ti == len(QTS) - 1:
                    ring.done(1)
                p.wait("vector", "ten")
                p.run("vector", lambda e, ti=ti, rows=rows, ob=ob, pso=pso: e.tensor_add(
                    x1_acc[0:rows, ti * C + ob * 512: ti * C + ob * 512 + 512],
                    x1_acc[0:rows, ti * C + ob * 512: ti * C + ob * 512 + 512],
                    pso[0:rows, 0:512]), "vec")
                p.fence("vector", "vec")
        p.barrier()
        es_p4y.close()
        p.tap("x1", x1_acc[:], (128, len(QTS) * C), F32)
        for ti, (t0, rows) in enumerate(QTS):
            p.dma("sync", s_x1[t0:t0 + rows, :], x1_acc[0:rows, ti * C:(ti + 1) * C], "tdma")
        p.wait("sync", "tdma")

        es_p5t = contextlib.ExitStack()
        t5a = es_p5t.enter_context(nc.sbuf_tensor("t5a", [128, C], F32, side="right"))
        p.claim(t5a)
        t5b = es_p5t.enter_context(nc.sbuf_tensor("t5b", [128, C], F32, side="right"))
        p.claim(t5b)
        t5i = es_p5t.enter_context(nc.sbuf_tensor("t5i", [128, C], I32, side="right"))
        p.claim(t5i)
        t5bf = es_p5t.enter_context(nc.sbuf_tensor("t5bf", [128, C], BF16, side="right"))
        p.claim(t5bf)
        for ti, (t0, rows) in enumerate(QTS):
            xa = x1_acc[0:rows, ti * C:(ti + 1) * C]
            p.wait("vector", "tdma")
            p.wait("scalar", "tdma")
            p.run("vector", lambda e, xa=xa, rows=rows: e.reduce_sum(st1[0:rows], xa, axis=AX.X), "vec")
            p.run("scalar", lambda e, xa=xa, rows=rows: e.activation(t5bf[0:rows, :], xa, AF.Square,
                                                                     accum_out=st2[0:rows]), "scl")
            p.wait("vector", "vec")
            p.wait("vector", "scl")
            p.run("vector", lambda e, rows=rows: e.tensor_single_scalar(st1[0:rows], st1[0:rows], 1.0 / C, ALU.mult), "vec")
            p.fence("vector", "vec")
            p.run("vector", lambda e, rows=rows: e.tensor_tensor(st3[0:rows], st1[0:rows], st1[0:rows], ALU.mult), "vec")
            p.run("vector", lambda e, rows=rows: e.tensor_scalar(st2[0:rows], st2[0:rows], 1.0 / C, 1e-5, ALU.mult, ALU.add), "vec")
            p.fence("vector", "vec")
            p.run("vector", lambda e, rows=rows: e.tensor_sub(st2[0:rows], st2[0:rows], st3[0:rows]), "vec")
            p.fence("vector", "vec")
            p.wait("scalar", "vec")
            p.run("scalar", lambda e, rows=rows: e.activation(st2[0:rows], st2[0:rows], AF.Sqrt), "scl")
            p.wait("vector", "scl")
            p.run("vector", lambda e, rows=rows: e.reciprocal(st2[0:rows], st2[0:rows]), "vec")
            p.fence("vector", "vec")
            p.run("vector", lambda e, xa=xa, rows=rows: e.tensor_scalar(t5a[0:rows, :], xa, st1[0:rows], None, ALU.subtract), "vec")
            p.fence("vector", "vec")
            p.run("vector", lambda e, rows=rows: e.tensor_scalar(t5a[0:rows, :], t5a[0:rows, :], st2[0:rows], None, ALU.mult), "vec")
            p.fence("vector", "vec")
            p.run("vector", lambda e, rows=rows: e.tensor_copy(t5bf[0:rows, :], t5a[0:rows, :]), "vec")
            p.fence("vector", "vec")
            p.wait("sync", "vec")
            p.dma("sync", s_xn1[t0:t0 + rows, :], t5bf[0:rows, :], "tdma")
            p.wait("scalar", "tdma")
            p.run("scalar", lambda e, rows=rows: e.activation(t5bf[0:rows, :], t5a[0:rows, :], AF.Square,
                                                              accum_out=st2[0:rows]), "scl")
            p.run("vector", lambda e, rows=rows: e.reduce_max(st3[0:rows], t5a[0:rows, :], axis=AX.X,
                                                              apply_absolute_value=True), "vec")
            p.wait("vector", "scl")
            p.fence("vector", "vec")
            p.run("vector", lambda e, rows=rows: e.tensor_scalar(st2[0:rows], st2[0:rows], 1.0 / C, 1e-6, ALU.mult, ALU.add), "vec")
            p.fence("vector", "vec")
            p.wait("scalar", "vec")
            p.run("scalar", lambda e, rows=rows: e.activation(st2[0:rows], st2[0:rows], AF.Sqrt), "scl")
            p.wait("vector", "scl")
            p.run("vector", lambda e, rows=rows: e.reciprocal(st2[0:rows], st2[0:rows]), "vec")
            p.fence("vector", "vec")
            p.run("vector", lambda e, rows=rows: e.tensor_single_scalar(st2[0:rows], st2[0:rows], C ** -0.5, ALU.mult), "vec")
            p.fence("vector", "vec")
            p.run("vector", lambda e, rows=rows: e.tensor_tensor(st3[0:rows], st3[0:rows], st2[0:rows], ALU.mult), "vec")
            p.fence("vector", "vec")
            p.run("vector", lambda e, rows=rows: e.tensor_scalar_max(st3[0:rows], st3[0:rows], 1e-5), "vec")
            p.fence("vector", "vec")
            p.run("vector", lambda e, rows=rows: e.reciprocal(st3[0:rows], st3[0:rows]), "vec")
            p.fence("vector", "vec")
            p.run("vector", lambda e, rows=rows: e.tensor_single_scalar(st3[0:rows], st3[0:rows], 127.0, ALU.mult), "vec")
            p.fence("vector", "vec")
            p.run("vector", lambda e, ti=ti, rows=rows: e.tensor_copy(qsc1[0:rows, ti:ti + 1], st3[0:rows]), "vec")
            p.fence("vector", "vec")
            p.wait("sync", "vec")
            p.dma("sync", s_qsc[t0:t0 + rows, :], st3[0:rows], "tdma")
            p.wait("vector", "tdma")
            p.run("vector", lambda e, rows=rows: e.tensor_tensor(st3[0:rows], st3[0:rows], st2[0:rows], ALU.mult), "vec")
            p.fence("vector", "vec")
            p.run("vector", lambda e, rows=rows: e.tensor_scalar(t5a[0:rows, :], t5a[0:rows, :], st3[0:rows], None, ALU.mult), "vec")
            p.fence("vector", "vec")
            p.run("vector", lambda e, rows=rows: e.tensor_copy(t5i[0:rows, :], t5a[0:rows, :]), "vec")
            p.fence("vector", "vec")
            p.run("vector", lambda e, rows=rows: e.tensor_copy(t5b[0:rows, :], t5i[0:rows, :]), "vec")
            p.fence("vector", "vec")
            p.run("vector", lambda e, rows=rows: e.tensor_sub(t5a[0:rows, :], t5a[0:rows, :], t5b[0:rows, :]), "vec")
            p.fence("vector", "vec")
            p.run("vector", lambda e, rows=rows: e.scalar_tensor_tensor(t5b[0:rows, :], t5a[0:rows, :], 0.5, t5b[0:rows, :], ALU.is_ge, ALU.add), "vec")
            p.fence("vector", "vec")
            p.run("vector", lambda e, rows=rows: e.tensor_single_scalar(t5a[0:rows, :], t5a[0:rows, :], -0.5, ALU.is_le), "vec")
            p.fence("vector", "vec")
            p.run("vector", lambda e, rows=rows: e.tensor_tensor(t5bf[0:rows, :], t5b[0:rows, :], t5a[0:rows, :], ALU.subtract), "vec")
            p.fence("vector", "vec")
            p.wait("sync", "vec")
            p.dma("sync", s_xq1[max(0, t0 - LB):max(0, t0 - LB) + rows - (LB if ti == 0 else 0), :],
                  t5bf[(LB if ti == 0 else 0):rows, :], "tdma")
        p.barrier()
        es_p5t.close()
        es_x1.close()

        es_p5x = contextlib.ExitStack()
        gateb = es_p5x.enter_context(nc.sbuf_tensor("gateb", [128, EXT], F32, side="right"))
        p.claim(gateb)
        p.dma("sync", c_td[:], td, "sdma")
        p.dma("sync", c_mk[:], mixk, "sdma")
        p.dma("sync", c_mr[:], mixr, "sdma")
        p.dma("gpsimd", c_sel[:], sel_prev.broadcast_to([128, 2]), "gdma")
        p.dma("gpsimd", gateb[:], scan_gate.broadcast_to([128, EXT]), "gdma")
        p.wait("scalar", "sdma")
        p.run("scalar", lambda e: e.activation(c_w[:], c_td[:], AF.Exp), "scl")
        p.fence("scalar", "scl")
        p.run("scalar", lambda e: e.activation(c_w[:], c_w[:], AF.Exp, scale=-1.0), "scl")
        p.wait("vector", "scl")

        xk = es_p5x.enter_context(nc.sbuf_tensor("xk", [128, NCC * EXT], BF16, side="right"))
        p.claim(xk)
        xr = es_p5x.enter_context(nc.sbuf_tensor("xr", [128, NCC * OWN], BF16, side="right"))
        p.claim(xr)
        xn1c = [es_p5x.enter_context(nc.sbuf_tensor(f"xn1c{i}", [128, EXT], BF16, side="right"))
                for i in range(2)]
        for _t in xn1c:
            p.claim(_t)
        t5junk = es_p5x.enter_context(nc.sbuf_tensor("t5junk", [128, EXT], BF16, side="right"))
        p.claim(t5junk)
        for i in range(2):
            p.new_sem(f"xnd{i}")
        p.wait("vector", "gdma")
        for cc in range(NCC):
            cslot2 = cc % 2
            if cc >= 2:
                p.wait("sync", "vec")
            p.dma_tr("sync", xn1c[cslot2][:], s_xn1[:, cc * 128:(cc + 1) * 128],
                     f"xnd{cslot2}")
            p.wait("vector", f"xnd{cslot2}", 16 * (cc // 2 + 1))
            xn1f = xn1c[cslot2]
            base = 0
            for (dst, dof, mixc) in ((xk, cc * EXT, c_mk), (xr, cc * OWN, c_mr)):
                own0 = 0 if dst is xk else LB
                width = EXT if dst is xk else OWN
                p.run("vector", lambda e, xn1f=xn1f, mixc=mixc, cc=cc: e.tensor_scalar(
                    st1[:], mixc[:, cc:cc + 1], -1.0, 1.0, ALU.mult, ALU.add), "vec")
                p.fence("vector", "vec")
                p.run("vector", lambda e, xn1f=xn1f, dst=dst, dof=dof, base=base, own0=own0, width=width, cc=cc: e.tensor_scalar(
                    dst[:, dof:dof + width], xn1f[:, base + own0: base + own0 + width],
                    st1[:], None, ALU.mult), "vec")
                p.fence("vector", "vec")
                sh0 = base + own0 - 1
                o0 = 0
                if own0 == 0:
                    sh0, o0 = base, 1
                w_eff = width - o0
                p.run("vector", lambda e, xn1f=xn1f, dst=dst, dof=dof, sh0=sh0, o0=o0, w_eff=w_eff, mixc=mixc, cc=cc: e.scalar_tensor_tensor(
                    t5junk[:, o0:o0 + w_eff], xn1f[:, sh0: sh0 + w_eff],
                    mixc[:, cc:cc + 1], dst[:, dof + o0:dof + o0 + w_eff], ALU.mult, ALU.add), "vec")
                p.fence("vector", "vec")
                p.run("vector", lambda e, dst=dst, dof=dof, o0=o0, w_eff=w_eff: e.tensor_copy(
                    dst[:, dof + o0:dof + o0 + w_eff], t5junk[:, o0:o0 + w_eff]), "vec")
                p.fence("vector", "vec")
                fix = LB - own0
                p.run("vector", lambda e, xn1f=xn1f, cc=cc, base=base: e.tensor_mul(
                    st1[:], xn1f[:, base + LB - 1: base + LB], c_sel[:, 0:1]), "vec")
                p.fence("vector", "vec")
                p.run("vector", lambda e, xn1f=xn1f, cc=cc, base=base: e.scalar_tensor_tensor(
                    st1[:], xn1f[:, base + LB + 1: base + LB + 2], c_sel[:, 1:2],
                    st1[:], ALU.mult, ALU.add), "vec")
                p.fence("vector", "vec")
                p.run("vector", lambda e, xn1f=xn1f, cc=cc, base=base: e.tensor_sub(
                    st1[:], st1[:], xn1f[:, base + LB: base + LB + 1]), "vec")
                p.fence("vector", "vec")
                p.run("vector", lambda e, xn1f=xn1f, mixc=mixc, cc=cc: e.tensor_scalar(
                    st1[:], st1[:], mixc[:, cc:cc + 1], None, ALU.mult), "vec")
                p.fence("vector", "vec")
                p.run("vector", lambda e, xn1f=xn1f, dst=dst, dof=dof, fix=fix, base=base, cc=cc: e.tensor_add(
                    dst[:, dof + fix: dof + fix + 1], st1[:],
                    xn1f[:, base + LB: base + LB + 1]), "vec")
                p.fence("vector", "vec")
                if dst is xk:
                    p.run("vector", lambda e, xn1f=xn1f, dst=dst, dof=dof, base=base: e.tensor_copy(
                        dst[:, dof: dof + 1], xn1f[:, base: base + 1]), "vec")
                    p.fence("vector", "vec")
        p.barrier()
        p.tap("xk", xk[:], (128, NCC * EXT), BF16)
        p.tap("xr", xr[:], (128, NCC * OWN), BF16)

        ku = [es_p5x.enter_context(nc.sbuf_tensor(f"ku{i}", [128, EXT], F32, side="right"))
              for i in range(2)]
        for _t in ku:
            p.claim(_t)
        ubuf = es_p5x.enter_context(nc.sbuf_tensor("ubuf", [128, EXT], F32, side="right"))
        p.claim(ubuf)
        rbuf = es_p5x.enter_context(nc.sbuf_tensor("rbuf", [128, OWN], F32, side="right"))
        p.claim(rbuf)
        wgate = es_p5x.enter_context(nc.sbuf_tensor("wgate", [128, EXT], F32, side="right"))
        p.claim(wgate)
        kstg = [es_p5x.enter_context(nc.sbuf_tensor(f"kstg{i}", [128, OWN], BF16, side="right"))
                for i in range(2)]
        for _t in kstg:
            p.claim(_t)
        for i in range(2):
            p.new_sem(f"kpd{i}")
        n_kp = [0]
        OWN_SEGS = [(0, 512), (512, 512)]
        for hp in range(11):
            gk = ring.n_issued
            ring.load(l1Wk.rearrange("(cc p) o -> p cc o", p=128)
                      [:, :, hp * 256:(hp + 1) * 256], cols=4096)
            gr = ring.n_issued
            ring.load(l1Wr.rearrange("(cc p) o -> p cc o", p=128)
                      [:, :, hp * 256:(hp + 1) * 256], cols=4096)
            ring.wait_granule("tensor", gk)
            gkb = ring.bufs[gk % ring.nslots]
            grb = ring.bufs[gr % ring.nslots]
            for j in range(2):
                grp = 3 * j
                p.wait("tensor", "scl")

                def mmk1(e, j=j, grp=grp, gkb=gkb):
                    last = None
                    for cc in range(NCC):
                        for si, (s0, swd) in enumerate(EXT_SEGS):
                            last = e.matmul(pb[grp + si][0:128, 0:swd],
                                            gkb[:, cc * 256 + j * 128: cc * 256 + j * 128 + 128],
                                            xk[:, cc * EXT + s0: cc * EXT + s0 + swd],
                                            start=(cc == 0), stop=(cc == NCC - 1))
                    return last
                p.run("tensor", mmk1, "ten")
                p.wait("scalar", "ten")
                p.wait("scalar", "vec")
                for si, (s0, swd) in enumerate(EXT_SEGS):
                    p.run("scalar", lambda e, j=j, grp=grp, si=si, s0=s0, swd=swd: e.activation(
                        ku[j][:, s0:s0 + swd], pb[grp + si][0:128, 0:swd], AF.Relu), "scl")
            ring.done(1)
            for j in range(2):
                hc = hp * 2 + j
                ring.wait_granule("tensor", gr)
                p.wait("tensor", "scl")
                p.wait("tensor", "vec")

                def mmr1(e, j=j, grb=grb):
                    last = None
                    for cc in range(NCC):
                        for si, (s0, swd) in enumerate(OWN_SEGS):
                            last = e.matmul(pb[6 + si][0:128, 0:swd],
                                            grb[:, cc * 256 + j * 128: cc * 256 + j * 128 + 128],
                                            xr[:, cc * OWN + s0: cc * OWN + s0 + swd],
                                            start=(cc == 0), stop=(cc == NCC - 1))
                    return last
                p.run("tensor", mmr1, "ten")
                if j == 1:
                    ring.done(1)
                p.wait("scalar", "ten")
                p.wait("scalar", "vec")
                for si, (s0, swd) in enumerate(OWN_SEGS):
                    p.run("scalar", lambda e, si=si, s0=s0, swd=swd: e.activation(
                        rbuf[:, s0:s0 + swd], pb[6 + si][0:128, 0:swd], AF.Copy), "scl")
                p.wait("vector", "scl")
                p.run("vector", lambda e, hc=hc: e.tensor_scalar(
                    wgate[:], gateb[:], c_w[:, hc:hc + 1], None, ALU.mult), "vec")
                p.fence("vector", "vec")
                if hp == 0 and j == 0:
                    p.wait("sync", "scl")
                    p.wait("sync", "vec")
                    p.tap("ku0", ku[0][:], (128, EXT), F32)
                    p.tap("rb0", rbuf[:], (128, OWN), F32)
                    p.tap("wg0", wgate[:], (128, EXT), F32)
                p.run("vector", lambda e, j=j: e.tensor_tensor_scan(
                    ubuf[:], wgate[:], ku[j][:], 0.0, ALU.mult, ALU.add), "vec")
                p.fence("vector", "vec")
                if hp == 0 and j == 0:
                    p.wait("sync", "vec")
                    p.tap("ub0", ubuf[:], (128, EXT), F32)
                kslot = n_kp[0] % 2
                if n_kp[0] >= 2:
                    p.wait("vector", f"kpd{kslot}", 16 * (n_kp[0] // 2))
                p.run("vector", lambda e, kslot=kslot: e.tensor_mul(
                    kstg[kslot][:], ubuf[:, LB:], rbuf[:]), "vec")
                p.fence("vector", "vec")
                p.wait("sync", "vec")
                p.dma("sync", s_krp[hc * 128:(hc + 1) * 128, :], kstg[kslot][:],
                      f"kpd{kslot}")
                n_kp[0] += 1
        p.barrier()
        es_p5x.close()

        es_p5o = contextlib.ExitStack()
        att1 = es_p5o.enter_context(nc.sbuf_tensor("att1", [128, 8 * C], F32, side="right"))
        p.claim(att1)
        jnk5 = es_p5o.enter_context(nc.sbuf_tensor("jnk5", [128, C], BF16, side="right"))
        p.claim(jnk5)
        gamb = es_p5o.enter_context(nc.sbuf_tensor("gamb", [128, C], F32, side="right"))
        p.claim(gamb)
        betb = es_p5o.enter_context(nc.sbuf_tensor("betb", [128, C], F32, side="right"))
        p.claim(betb)
        p.dma("gpsimd", gamb[:], gam.broadcast_to([128, C]), "gdma")
        p.dma("gpsimd", betb[:], bet.broadcast_to([128, C]), "gdma")
        HCGS = [(0, 4), (4, 4), (8, 4), (12, 4), (16, 4), (20, 2)]
        for ob in range(4):
            for gidx, (h0, nhc_g) in enumerate(HCGS):
                gk2 = ring.n_issued
                ring.load(s_krp.rearrange("(hc p) t -> p hc t", p=128)
                          [:, h0:h0 + nhc_g, :], cols=nhc_g * OWN)
                gi = ring.n_issued
                ring.load(l1Wv.rearrange("(hc p) o -> p hc o", p=128)
                          [:, h0:h0 + nhc_g, ob * 512:(ob + 1) * 512], cols=nhc_g * 512)
                ring.wait_granule("tensor", gk2)
                ring.wait_granule("tensor", gi)
                kbuf = ring.bufs[gk2 % ring.nslots]
                gbuf = ring.bufs[gi % ring.nslots]
                for ti in range(8):
                    if gidx == 0:
                        p.wait("tensor", "vec")

                    def mmov(e, ti=ti, nhc_g=nhc_g, kbuf=kbuf, gbuf=gbuf, gidx=gidx):
                        last = None
                        for hj in range(nhc_g):
                            last = e.matmul(pb[ti][0:128, 0:512],
                                            kbuf[:, hj * OWN + ti * 128: hj * OWN + ti * 128 + 128],
                                            gbuf[:, hj * 512:(hj + 1) * 512],
                                            start=(gidx == 0 and hj == 0),
                                            stop=(gidx == len(HCGS) - 1 and hj == nhc_g - 1))
                        return last
                    p.run("tensor", mmov, "ten")
                ring.done(2)
            p.wait("vector", "ten")
            for ti in range(8):
                p.run("vector", lambda e, ti=ti, ob=ob: e.tensor_copy(
                    att1[:, ti * C + ob * 512: ti * C + ob * 512 + 512],
                    pb[ti][0:128, 0:512]), "vec")
                p.fence("vector", "vec")
        p.wait("vector", "gdma")
        for ti in range(8):
            xa = att1[:, ti * C:(ti + 1) * C]
            p.run("vector", lambda e, xa=xa: e.reduce_sum(st1[:], xa, axis=AX.X), "vec")
            p.wait("scalar", "vec")
            p.run("scalar", lambda e, xa=xa: e.activation(jnk5[:], xa, AF.Square,
                                                          accum_out=st2[:]), "scl")
            p.wait("vector", "scl")
            p.fence("vector", "vec")
            p.run("vector", lambda e: e.tensor_single_scalar(st1[:], st1[:], 1.0 / C, ALU.mult), "vec")
            p.fence("vector", "vec")
            p.run("vector", lambda e: e.tensor_tensor(st3[:], st1[:], st1[:], ALU.mult), "vec")
            p.run("vector", lambda e: e.tensor_scalar(st2[:], st2[:], 1.0 / C, 1e-5, ALU.mult, ALU.add), "vec")
            p.fence("vector", "vec")
            p.run("vector", lambda e: e.tensor_sub(st2[:], st2[:], st3[:]), "vec")
            p.fence("vector", "vec")
            p.wait("scalar", "vec")
            p.run("scalar", lambda e: e.activation(st2[:], st2[:], AF.Sqrt), "scl")
            p.wait("vector", "scl")
            p.run("vector", lambda e: e.reciprocal(st2[:], st2[:]), "vec")
            p.fence("vector", "vec")
            p.run("vector", lambda e, xa=xa: e.tensor_scalar(xa, xa, st1[:], None, ALU.subtract), "vec")
            p.fence("vector", "vec")
            p.run("vector", lambda e, xa=xa: e.tensor_scalar(xa, xa, st2[:], None, ALU.mult), "vec")
            p.fence("vector", "vec")
            p.run("vector", lambda e, xa=xa: e.tensor_mul(xa, xa, gamb[:]), "vec")
            p.fence("vector", "vec")
            p.run("vector", lambda e, xa=xa: e.tensor_add(xa, xa, betb[:]), "vec")
            p.fence("vector", "vec")
            p.wait("sync", "vec")
            p.dma("sync", s_att1[ti * 128:(ti + 1) * 128, :], xa, "tdma")
        p.barrier()
        es_p5o.close()
        p.tap("att1", att1[:], (128, 8 * C), F32)

        es_sg = contextlib.ExitStack()
        ringf = WRing(p, "wrf", 2, 4096, dtype=F32, es=es_sg)
        sgstg = [es_sg.enter_context(nc.sbuf_tensor(f"sgstg{i}", [128, 4096], BF16, side="right"))
                 for i in range(2)]
        for _t in sgstg:
            p.claim(_t)
        for i in range(2):
            p.new_sem(f"sgd{i}")
        p.run("vector", lambda e: e.memset(acc1[:], 0.0), "vec")
        p.run("vector", lambda e: e.memset(acc2[:], 0.0), "vec")
        p.fence("vector", "vec")
        n_sg = [0]

        def sgn_gen(w_ap, n_cg, n_og, dst_scr, acc):
            for cg in range(n_cg):
                for og in range(n_og):
                    gi = ringf.n_issued
                    ringf.load(w_ap.rearrange("(cg p) o -> p cg o", p=128)
                               [:, cg * 8:(cg + 1) * 8, og * 512:(og + 1) * 512], cols=4096)
                    slot = n_sg[0] % 2
                    if n_sg[0] >= 2:
                        p.wait("scalar", f"sgd{slot}", 16 * (n_sg[0] // 2))
                    ringf.wait_granule("scalar", gi)
                    p.run("scalar", lambda e, slot=slot, gb=ringf.bufs[gi % ringf.nslots]: e.activation(
                        sgstg[slot][:], gb[:], AF.Sign), "scl")
                    ringf.wait_granule("vector", gi)
                    p.run("vector", lambda e, gb=ringf.bufs[gi % ringf.nslots]: e.reduce_sum(
                        st1[:], gb[:], axis=AX.X, apply_absolute_value=True), "vec")
                    p.fence("vector", "vec")
                    p.run("vector", lambda e, acc=acc: e.tensor_add(acc[:], acc[:], st1[:]), "vec")
                    p.fence("vector", "vec")
                    ringf.free_at.append([("scl", p.val("scl")), ("vec", p.val("vec"))])
                    p.wait("sync", "scl")
                    p.wait("sync", "vec")
                    p.dma("sync",
                          dst_scr.rearrange("(cg p) o -> p cg o", p=128)
                          [:, cg * 8:(cg + 1) * 8, og * 512:(og + 1) * 512],
                          sgstg[slot][:].rearrange("p (cg o) -> p cg o", cg=8),
                          f"sgd{slot}")
                    n_sg[0] += 1

        sgn_gen(f1w, 2, 16, s_sgn1, acc1)
        sgn_gen(f2w, 8, 4, s_sgn2, acc2)
        p.barrier()
        es_sg.close()
        p.run("vector", lambda e: e.tensor_copy(sw12[:, 0:1], acc1[:]), "vec")
        p.run("vector", lambda e: e.tensor_copy(sw12[:, 1:2], acc2[:]), "vec")
        p.run("vector", lambda e: e.memset(c_onesc[:], 1.0), "vec")
        p.run("vector", lambda e: e.memset(c_onesr[:], 1.0), "vec")
        p.fence("vector", "vec")
        p.wait("tensor", "vec")
        p.run("tensor", lambda e: e.matmul(pb[0][0:1, 0:2], c_onesc[:, 0:1],
                                           sw12[:, 0:2], start=True, stop=True), "ten")
        p.wait("scalar", "ten")
        p.run("scalar", lambda e: e.activation(sw12[0:1, 2:4], pb[0][0:1, 0:2],
                                               AF.Copy, scale=1.0 / (C * INNER)), "scl")
        p.wait("tensor", "scl")
        p.run("tensor", lambda e: e.matmul(pb[1][0:128, 0:2], c_onesr[0:1, :],
                                           sw12[0:1, 2:4], start=True, stop=True), "ten")
        p.wait("scalar", "ten")
        p.run("scalar", lambda e: e.activation(sw12[:, 0:2], pb[1][0:128, 0:2],
                                               AF.Copy), "scl")
        p.barrier()

        es_p6a = contextlib.ExitStack()
        xq1f = es_p6a.enter_context(nc.sbuf_tensor("xq1f", [128, NCC * 256], BF16, side="right"))
        p.claim(xq1f)
        hbuf = es_p6a.enter_context(nc.sbuf_tensor("hbuf", [128, 2 * INNER], F32, side="right"))
        p.claim(hbuf)
        h6i = es_p6a.enter_context(nc.sbuf_tensor("h6i", [128, C], I32, side="right"))
        p.claim(h6i)
        h6b = es_p6a.enter_context(nc.sbuf_tensor("h6b", [128, C], F32, side="right"))
        p.claim(h6b)
        h6bf = es_p6a.enter_context(nc.sbuf_tensor("h6bf", [128, INNER], BF16, side="right"))
        p.claim(h6bf)
        b1b = es_p6a.enter_context(nc.sbuf_tensor("b1b", [128, INNER], BF16, side="right"))
        p.claim(b1b)
        p.dma("gpsimd", b1b[:], f1b.broadcast_to([128, INNER]), "gdma")
        sfap = [es_p6a.enter_context(nc.sbuf_tensor(f"sfap{i}", [128, 1], F32, side="right"))
                for i in range(2)]
        for _t in sfap:
            p.claim(_t)
        for tp in range(4):
            p.wait("sync", "ten")
            for cc in range(NCC):
                p.dma_tr("sync", xq1f[:, cc * 256:(cc + 1) * 256],
                         s_xq1[tp * 256:(tp + 1) * 256, cc * 128:(cc + 1) * 128], "tdma")
            for tj in range(2):
                ti = tp * 2 + tj
                p.wait("sync", "vec")
                p.dma("sync", sfap[tj][:], s_qsc[LB + ti * 128: LB + (ti + 1) * 128, :], "sdma")
            p.wait("vector", "sdma")
            for tj in range(2):
                p.run("vector", lambda e, tj=tj: e.reciprocal(sfap[tj][:], sfap[tj][:]), "vec")
                p.fence("vector", "vec")
                p.run("vector", lambda e, tj=tj: e.tensor_scalar(
                    sfap[tj][:], sfap[tj][:], sw12[:, 0:1], None, ALU.mult), "vec")
                p.fence("vector", "vec")
            p.wait("tensor", "tdma")
            evac_vec = {}
            for ob in range(16):
                for ch in range(2):
                    if ch == 0 and ob >= 2:
                        p.wait("tensor", "vec", evac_vec[ob - 2])
                    gi = ring.n_issued
                    ring.load(s_sgn1.rearrange("(cg p) o -> p cg o", p=128)
                              [:, ch * 8:(ch + 1) * 8, ob * 512:(ob + 1) * 512], cols=4096)
                    ring.wait_granule("tensor", gi)
                    gbuf = ring.bufs[gi % ring.nslots]
                    for tj in range(2):
                        ti = tp * 2 + tj
                        psb = pb[tj * 2 + (ob % 2)]
                        if ch == 0 and ob < 2:
                            p.wait("tensor", "vec")
                            p.wait("tensor", "scl")

                        def mm61(e, ch=ch, tj=tj, ti=ti, psb=psb, gbuf=gbuf):
                            last = None
                            for c8 in range(8):
                                cc = ch * 8 + c8
                                last = e.matmul(psb[0:128, 0:512],
                                                xq1f[:, cc * 256 + tj * 128: cc * 256 + tj * 128 + 128],
                                                gbuf[:, c8 * 512:(c8 + 1) * 512],
                                                start=(cc == 0), stop=(cc == NCC - 1))
                            return last
                        p.run("tensor", mm61, "ten")
                    ring.done(1)
                p.wait("vector", "ten")
                for tj in range(2):
                    ti = tp * 2 + tj
                    psb = pb[tj * 2 + (ob % 2)]
                    p.run("vector", lambda e, tj=tj, ob=ob, psb=psb: e.tensor_scalar(
                        hbuf[:, tj * INNER + ob * 512: tj * INNER + ob * 512 + 512],
                        psb[0:128, 0:512], sfap[tj][:], None, ALU.mult), "vec")
                    p.fence("vector", "vec")
                evac_vec[ob] = p.val("vec")
            for tj in range(2):
                ti = tp * 2 + tj
                hb = hbuf[:, tj * INNER:(tj + 1) * INNER]
                p.wait("vector", "gdma")
                p.run("vector", lambda e, hb=hb: e.tensor_add(hb, hb, b1b[:]), "vec")
                p.fence("vector", "vec")
                p.wait("scalar", "vec")
                p.run("scalar", lambda e, hb=hb: e.activation(hb, hb, AF.Gelu), "scl")
                p.fence("scalar", "scl")
                p.wait("vector", "scl")
                p.wait("scalar", "tdma")
                p.run("scalar", lambda e, hb=hb: e.activation(h6bf[:], hb, AF.Square,
                                                              accum_out=st2[:]), "scl")
                p.run("vector", lambda e, hb=hb: e.reduce_max(st3[:], hb, axis=AX.X,
                                                              apply_absolute_value=True), "vec")
                p.wait("vector", "scl")
                p.fence("vector", "vec")
                p.run("vector", lambda e: e.tensor_scalar(st2[:], st2[:], 1.0 / INNER, 1e-6, ALU.mult, ALU.add), "vec")
                p.fence("vector", "vec")
                p.wait("scalar", "vec")
                p.run("scalar", lambda e: e.activation(st2[:], st2[:], AF.Sqrt), "scl")
                p.wait("vector", "scl")
                p.run("vector", lambda e: e.reciprocal(st2[:], st2[:]), "vec")
                p.fence("vector", "vec")
                p.run("vector", lambda e: e.tensor_single_scalar(st2[:], st2[:], INNER ** -0.5, ALU.mult), "vec")
                p.fence("vector", "vec")
                p.run("vector", lambda e: e.tensor_tensor(st3[:], st3[:], st2[:], ALU.mult), "vec")
                p.fence("vector", "vec")
                p.run("vector", lambda e: e.tensor_scalar_max(st3[:], st3[:], 1e-5), "vec")
                p.fence("vector", "vec")
                p.run("vector", lambda e: e.reciprocal(st3[:], st3[:]), "vec")
                p.fence("vector", "vec")
                p.run("vector", lambda e: e.tensor_single_scalar(st3[:], st3[:], 127.0, ALU.mult), "vec")
                p.fence("vector", "vec")
                p.run("vector", lambda e, ti=ti: e.tensor_copy(qsc2[:, ti:ti + 1], st3[:]), "vec")
                p.run("vector", lambda e: e.tensor_tensor(st3[:], st3[:], st2[:], ALU.mult), "vec")
                p.fence("vector", "vec")
                p.run("vector", lambda e, hb=hb: e.tensor_scalar(hb, hb, st3[:], None, ALU.mult), "vec")
                p.fence("vector", "vec")
                for ck in range(INNER // C):
                    hc_ = hb[:, ck * C:(ck + 1) * C]
                    p.run("vector", lambda e, hc_=hc_: e.tensor_copy(h6i[:], hc_), "vec")
                    p.fence("vector", "vec")
                    p.run("vector", lambda e: e.tensor_copy(h6b[:], h6i[:]), "vec")
                    p.fence("vector", "vec")
                    p.run("vector", lambda e, hc_=hc_: e.tensor_sub(hc_, hc_, h6b[:]), "vec")
                    p.fence("vector", "vec")
                    p.run("vector", lambda e, hc_=hc_: e.scalar_tensor_tensor(h6b[:], hc_, 0.5, h6b[:], ALU.is_ge, ALU.add), "vec")
                    p.fence("vector", "vec")
                    p.run("vector", lambda e, hc_=hc_: e.tensor_single_scalar(hc_, hc_, -0.5, ALU.is_le), "vec")
                    p.fence("vector", "vec")
                    p.run("vector", lambda e, hc_=hc_, ck=ck: e.tensor_tensor(
                        h6bf[:, ck * C:(ck + 1) * C], h6b[:], hc_, ALU.subtract), "vec")
                    p.fence("vector", "vec")
                p.wait("sync", "vec")
                p.dma("sync", s_xq2[ti * 128:(ti + 1) * 128, :], h6bf[:], "tdma")
        p.barrier()
        es_p6a.close()

        es_p6b = contextlib.ExitStack()
        xq2f = es_p6b.enter_context(nc.sbuf_tensor("xq2f", [128, 64 * 256], BF16, side="right"))
        p.claim(xq2f)
        b2b = es_p6b.enter_context(nc.sbuf_tensor("b2b", [128, C], BF16, side="right"))
        p.claim(b2b)
        xt1 = es_p6b.enter_context(nc.sbuf_tensor("xt1", [128, C], F32, side="right"))
        p.claim(xt1)
        at1 = es_p6b.enter_context(nc.sbuf_tensor("at1", [128, C], F32, side="right"))
        p.claim(at1)
        of32 = es_p6b.enter_context(nc.sbuf_tensor("of32", [128, C], F32, side="right"))
        p.claim(of32)
        p.dma("gpsimd", b2b[:], f2b.broadcast_to([128, C]), "gdma")
        for tp in range(4):
            p.wait("sync", "ten")
            for ic in range(64):
                p.dma_tr("sync", xq2f[:, ic * 256:(ic + 1) * 256],
                         s_xq2[tp * 256:(tp + 1) * 256, ic * 128:(ic + 1) * 128], "tdma")
            p.wait("tensor", "tdma")
            for icg in range(8):
                for ob in range(4):
                    gi = ring.n_issued
                    ring.load(s_sgn2.rearrange("(cg p) o -> p cg o", p=128)
                              [:, icg * 8:(icg + 1) * 8, ob * 512:(ob + 1) * 512], cols=4096)
                    ring.wait_granule("tensor", gi)
                    gbuf = ring.bufs[gi % ring.nslots]
                    if icg == 0 and ob == 0:
                        p.wait("tensor", "vec")

                    def mm62(e, icg=icg, ob=ob, gbuf=gbuf):
                        last = None
                        for tj in range(2):
                            for i8 in range(8):
                                ic = icg * 8 + i8
                                last = e.matmul(pb[tj * 4 + ob][0:128, 0:512],
                                                xq2f[:, ic * 256 + tj * 128: ic * 256 + tj * 128 + 128],
                                                gbuf[:, i8 * 512:(i8 + 1) * 512],
                                                start=(ic == 0), stop=(ic == 63))
                        return last
                    p.run("tensor", mm62, "ten")
                    ring.done(1)
            for tj in range(2):
                ti = tp * 2 + tj
                p.wait("sync", "vec")
                p.dma("sync", xt1[:], s_x1[LB + ti * 128: LB + (ti + 1) * 128, :], "sdma")
                p.dma("sync", at1[:], s_att1[ti * 128:(ti + 1) * 128, :], "sdma")
                p.wait("vector", "ten")
                p.wait("vector", "sdma")
                p.wait("vector", "gdma")
                p.run("vector", lambda e, ti=ti: e.tensor_copy(st3[:], qsc2[:, ti:ti + 1]), "vec")
                p.fence("vector", "vec")
                p.run("vector", lambda e: e.reciprocal(st3[:], st3[:]), "vec")
                p.fence("vector", "vec")
                p.run("vector", lambda e: e.tensor_scalar(st3[:], st3[:], sw12[:, 1:2], None, ALU.mult), "vec")
                p.fence("vector", "vec")
                for ob in range(4):
                    p.run("vector", lambda e, tj=tj, ob=ob: e.tensor_scalar(
                        of32[:, ob * 512:(ob + 1) * 512], pb[tj * 4 + ob][0:128, 0:512],
                        st3[:], None, ALU.mult), "vec")
                    p.fence("vector", "vec")
                p.run("vector", lambda e: e.tensor_add(of32[:], of32[:], b2b[:]), "vec")
                p.fence("vector", "vec")
                p.run("vector", lambda e: e.tensor_add(of32[:], of32[:], at1[:]), "vec")
                p.fence("vector", "vec")
                p.run("vector", lambda e: e.tensor_add(of32[:], of32[:], xt1[:]), "vec")
                p.fence("vector", "vec")
                p.wait("sync", "vec")
                p.dma("sync", out[ti * 128:(ti + 1) * 128, :], of32[:], "sdma")
        p.barrier()
        es_p6b.close()
        if "scores" in DEBUG_TAPS:
            p.tap("scores", sc_dbg[:], (128, 9 * E), F32)
        if "logits" in DEBUG_TAPS:
            p.tap("logits", lg_dbg[:], (128, 9 * E), F32)
        p.tap("cw", cw_t[:], (128, len(QTS) * E), F32)
        p.tap("xne", xne_fm[:], (128, NCC * EXT), BF16)
        p.tap("xnf", xnf_fm[:], (128, NCC * T), BF16)

        p.wait("sync", "sdma")
        p.wait("sync", "tdma")
        p.wait("sync", "gdma")

        p.finish()
    return nc


def kernel(**inputs):
    shared = host_prep(inputs)
    x = np.asarray(inputs["L0_attn_x"], np.float32)
    nc = build_program()
    in_maps = []
    for c in range(8):
        m = dict(shared)
        m.update(percore_prep(x, c))
        in_maps.append(m)
    res = run_bass_kernel_spmd(nc, in_maps, core_ids=list(range(8)))
    out = np.zeros((4, T, C), np.float32)
    for c in range(8):
        b, heavy = c // 2, (c % 2 == 0)
        rows = slice(1024, 2048) if heavy else slice(0, 1024)
        out[b, rows] = res.results[c]["out"]
    return out


# version 03
# speedup vs baseline: 1.0237x; 1.0237x over previous
import contextlib
import numpy as np
import ml_dtypes
import concourse.bass as bass
import concourse.mybir as mybir
from concourse.bass_utils import run_bass_kernel_spmd

F32 = mybir.dt.float32
BF16 = mybir.dt.bfloat16
I32 = mybir.dt.int32
U8 = mybir.dt.uint8
AF = mybir.ActivationFunctionType
ALU = mybir.AluOpType
AX = mybir.AxisListType

BF = ml_dtypes.bfloat16

T = 2048
C = 2048
EXT = 1088
LB = 64
OWN = 1024
NH = 16
HD = 128
NCC = C // 128
H1 = 2816
NHC = 22
INNER = 8192
E = 32
LAT = 1024
EINT = 1024
SHI = 4096

QCS = [(0, 512, 12), (512, 512, 16), (1024, 64, 16)]
MIX = [[0, 1, 2, 3, 7, 8, 9, 10, 11], [3, 4, 5, 6, 7, 11, 12, 13, 14, 15], [7, 15]]
N_MIX = sum(len(m) for m in MIX)

DEBUG_TAPS = ()

NEG = -1.0e30


def _slopes():
    import math
    start = 2 ** (-2 ** (-(math.log2(NH) - 3)))
    return np.array([start * start ** i for i in range(NH)], dtype=np.float64)


def host_prep(inp):
    d = {}
    f32 = np.float32

    def t2(a):
        return np.ascontiguousarray(np.asarray(a).T)

    d["Wq"] = np.ascontiguousarray(inp["L0_attn_Wq"], f32)
    d["Wk"] = np.ascontiguousarray(inp["L0_attn_Wk"], f32)
    d["Wv"] = np.ascontiguousarray(inp["L0_attn_Wv"], f32)
    d["Wo"] = np.ascontiguousarray(inp["L0_attn_Wo"], f32)
    for nm, key in (("bq", "L0_attn_bq"), ("bk", "L0_attn_bk")):
        d[nm] = np.ascontiguousarray(np.asarray(inp[key], f32).reshape(NCC, 128).T)
    d["bv"] = np.asarray(inp["L0_attn_bv"], f32).reshape(1, C)
    d["bo"] = np.asarray(inp["L0_attn_bo"], f32).reshape(1, C)
    d["gate_w"] = t2(inp["L0_ffn_gate_w"])
    d["gate_b"] = np.asarray(inp["L0_ffn_gate_bias"], f32).reshape(1, E)
    d["fc1"] = t2(inp["L0_ffn_fc1_w"])
    d["fc2"] = t2(inp["L0_ffn_fc2_w"])
    w1a = np.asarray(inp["L0_ffn_w1"], f32).transpose(0, 2, 1)
    d["w1g"] = np.ascontiguousarray(
        w1a.reshape(E, 8, 128, 2, 512).transpose(0, 3, 2, 1, 4).reshape(2 * E, 128, 4096))
    w2a = np.asarray(inp["L0_ffn_w2"], f32).transpose(0, 2, 1)
    d["w2g"] = np.ascontiguousarray(
        w2a.reshape(E, 8, 128, 2, 512).transpose(0, 3, 2, 1, 4).reshape(2 * E, 128, 4096))
    d["shu"] = t2(inp["L0_ffn_shared_up_w"])
    d["shd"] = t2(inp["L0_ffn_shared_down_w"])
    d["mixk"] = np.ascontiguousarray(np.asarray(inp["L1_attn_time_mix_k"], f32).reshape(NCC, 128).T)
    d["mixr"] = np.ascontiguousarray(np.asarray(inp["L1_attn_time_mix_r"], f32).reshape(NCC, 128).T)
    d["l1Wk"] = t2(inp["L1_attn_Wk"])
    d["l1Wr"] = t2(inp["L1_attn_Wr"])
    d["l1Wv"] = t2(inp["L1_attn_Wv"])
    td = np.asarray(inp["L1_attn_time_decay"], f32)
    d["td"] = np.ascontiguousarray(td.reshape(NHC, 128).T)
    d["gam"] = np.asarray(inp["L1_attn_ln_gamma"], f32).reshape(1, C)
    d["bet"] = np.asarray(inp["L1_attn_ln_beta"], f32).reshape(1, C)
    d["f1w"] = t2(inp["L1_ffn_w1"])
    d["f1b"] = np.asarray(inp["L1_ffn_b1"], f32).reshape(1, INNER)
    d["f2w"] = t2(inp["L1_ffn_w2"])
    d["f2b"] = np.asarray(inp["L1_ffn_b2"], f32).reshape(1, C)
    return d


def percore_prep(x, core):
    b, heavy = core // 2, (core % 2 == 0)
    f32 = np.float32
    xb = np.ascontiguousarray(np.asarray(x[b], f32))
    if heavy:
        q0 = 1024
        x_ext = np.ascontiguousarray(xb[q0 - LB:q0 + OWN])
        gpos = np.arange(EXT) + (q0 - LB)
        sel_prev = np.array([[1.0, 0.0]], f32)
    else:
        q0 = 0
        x_ext = np.zeros((EXT, C), f32)
        x_ext[LB:] = xb[0:OWN]
        gpos = np.concatenate([np.full(LB, -1), np.arange(OWN)])
        sel_prev = np.array([[0.0, 1.0]], f32)

    qmax = np.where(gpos < 0, 127, gpos)

    slopes = _slopes()
    kk = np.arange(T, dtype=np.float64)
    alibi = (-slopes[:, None] * (T - 1 - kk)[None, :]).astype(f32)

    nb = sum(e for (_, _, e) in QCS)
    bias = np.zeros((NH, 128, nb), f32)
    col = 0
    for qc, (qlo, qw, ext) in enumerate(QCS):
        qmax_chunk = qmax[qlo:qlo + qw].max()
        for kt in range(ext):
            ks = np.arange(kt * 128, (kt + 1) * 128)
            b_ = alibi[:, ks].copy()
            if kt * 128 > qmax_chunk:
                b_[:] = NEG
            bias[:, :, col] = b_
            col += 1

    masks = []
    for qc, (qlo, qw, ext) in enumerate(QCS):
        for kt in MIX[qc]:
            ks = np.arange(kt * 128, (kt + 1) * 128)
            m = np.where(ks[:, None] <= qmax[None, qlo:qlo + qw], 0.0, NEG).astype(f32)
            masks.append(m.astype(BF))
    mask0 = np.ascontiguousarray(np.concatenate(
        [m for m in masks if m.shape[1] == 512], axis=1))
    mask1 = np.ascontiguousarray(np.concatenate(
        [m for m in masks if m.shape[1] == 64], axis=1))

    gate = np.ones((1, EXT), f32)
    if not heavy:
        gate[0, LB] = 0.0

    qoff = (slopes[:, None] * (T - 1 - qmax[None, :].astype(np.float64))
            ).astype(f32).astype(BF)

    bias_dev = np.ascontiguousarray(bias.transpose(1, 0, 2).reshape(128, NH * nb))
    return dict(x_full=xb, x_ext=x_ext, attn_bias=bias_dev,
                mix0=mask0, mix1=mask1, scan_gate=gate,
                qoff=np.ascontiguousarray(qoff), sel_prev=sel_prev)



class P:

    ENGS = ("sync", "gpsimd", "tensor", "vector", "scalar")

    def __init__(self, nc, es):
        self.nc = nc
        self.es = es
        self.th = {e: [] for e in self.ENGS}
        self.sem = {}
        self.cnt = {}
        self.dma_chans = set()
        self.taps = {}

    def new_sem(self, name):
        h = self.es.enter_context(self.nc.semaphore(name))
        self.sem[name] = h
        self.cnt[name] = 0
        return h

    def inc(self, name, amt):
        self.cnt[name] += amt
        return self.cnt[name]

    def val(self, name):
        return self.cnt[name]

    def on(self, eng, fn):
        self.th[eng].append(fn)

    def dma(self, qeng, out_ap, in_ap, ch):
        sem = self.sem[ch]
        self.dma_chans.add(ch)
        self.inc(ch, 16)

        def f(eng, out_ap=out_ap, in_ap=in_ap):
            eng.dma_start(out=out_ap, in_=in_ap).then_inc(sem, 16)
        self.on(qeng, f)

    def dma_tr(self, qeng, out_ap, in_ap, ch):
        sem = self.sem[ch]
        self.dma_chans.add(ch)
        self.inc(ch, 16)

        def f(eng, out_ap=out_ap, in_ap=in_ap):
            eng.dma_start(out=out_ap, in_=in_ap, transpose=True).then_inc(sem, 16)
        self.on(qeng, f)

    def wait(self, eng, ch, val=None):
        sem = self.sem[ch]
        v = self.val(ch) if val is None else val
        assert v <= self.cnt[ch], f"wait {ch}>={v} but only {self.cnt[ch]} emitted"

        def f(e, sem=sem, v=v):
            e.wait_ge(sem, v)
        self.on(eng, f)

    def run(self, eng_name, fn, ch=None, amt=1, extra=None):
        if ch is None:
            self.on(eng_name, fn)
        else:
            sem = self.sem[ch]
            self.inc(ch, amt)

            def f(e, fn=fn, sem=sem, amt=amt, extra=extra):
                i = fn(e).then_inc(sem, amt)
                if extra is not None:
                    i.then_inc(extra[0], extra[1])
            self.on(eng_name, f)

    def fence(self, eng_name, ch):
        self.wait(eng_name, ch)

    def barrier(self):
        for ch in sorted(self.dma_chans):
            self.wait("sync", ch)
        for ch in ("ten", "vec", "scl", "gps"):
            self.wait("sync", ch)
        sem = self.sem["bar"]
        tgt = self.inc("bar", len(self.ENGS))
        for e in self.ENGS:
            def f(eng, sem=sem):
                eng.nop().then_inc(sem, 1)
            self.on(e, f)
        for e in self.ENGS:
            self.wait(e, "bar", tgt)

    def claim(self, t):
        self.run("vector", lambda e, t=t: e.memset(t[:], 0), "vec")
        self.fence("vector", "vec")
        self.wait("sync", "vec")
        self.wait("gpsimd", "vec")
        self.wait("scalar", "vec")
        self.wait("tensor", "vec")

    def tap(self, name, sbuf_ap, shape, dtype):
        if name not in DEBUG_TAPS:
            return
        dt = self.nc.dram_tensor("tap_" + name, list(shape), dtype,
                                 kind="ExternalOutput").ap()
        self.dma("sync", dt, sbuf_ap, "sdma")

    def finish(self):
        nc = self.nc
        with nc.Block() as block:
            for name in self.ENGS:
                lst = self.th[name]
                def runner(eng, lst=lst):
                    for f in lst:
                        f(eng)
                getattr(block, name)(runner)


class WRing:

    def __init__(self, p, name, nslots, cols, dtype=BF16, es=None):
        self.p = p
        self.nslots = nslots
        self.bufs = [(es or p.es).enter_context(p.nc.sbuf_tensor(f"{name}{i}", [128, cols], dtype))
                     for i in range(nslots)]
        self.ld = [p.new_sem(f"{name}_ld{i}") for i in range(nslots)]
        self.ldn = [f"{name}_ld{i}" for i in range(nslots)]
        self.free_at = []
        self.n_issued = 0

    def load(self, in_ap, cols=None):
        p = self.p
        i = self.n_issued
        slot = i % self.nslots
        self.n_issued += 1
        if i >= self.nslots:
            assert len(self.free_at) >= i - self.nslots + 1, "ring: missing done()"
            for ch, v in self.free_at[i - self.nslots]:
                p.wait("gpsimd", ch, v)
        buf = self.bufs[slot]
        out_ap = buf[:] if cols is None else buf[:, 0:cols]
        p.dma("gpsimd", out_ap, in_ap, self.ldn[slot])
        return slot, buf

    def wait_granule(self, eng_name, slot_i):
        p = self.p
        slot = slot_i % self.nslots
        times = slot_i // self.nslots + 1
        p.wait(eng_name, self.ldn[slot], 16 * times)

    def done(self, n=1):
        for _ in range(n):
            self.free_at.append([("ten", self.p.val("ten"))])


def ceil_div(a, b):
    return (a + b - 1) // b


def build_program():
    nc = bass.Bass("TRN2", target_bir_lowering=False, debug=False)
    es = contextlib.ExitStack()
    p = P(nc, es)
    def din(name, shape, dtype=F32):
        return nc.dram_tensor(name, list(shape), dtype, kind="ExternalInput").ap()

    x_full = din("x_full", (T, C))
    x_ext = din("x_ext", (EXT, C))
    attn_bias = din("attn_bias", (128, NH * sum(e for (_, _, e) in QCS)))
    mix0 = din("mix0", (128, 19 * 512), BF16)
    mix1 = din("mix1", (128, 2 * 64), BF16)
    scan_gate = din("scan_gate", (1, EXT))
    qoff = din("qoff", (NH, EXT), BF16)
    sel_prev = din("sel_prev", (1, 2))
    Wq, Wk, Wv, Wo = din("Wq", (C, C)), din("Wk", (C, C)), din("Wv", (C, C)), din("Wo", (C, C))
    bq, bk = din("bq", (128, NCC)), din("bk", (128, NCC))
    bv, bo = din("bv", (1, C)), din("bo", (1, C))
    gate_w, gate_b = din("gate_w", (C, E)), din("gate_b", (1, E))
    fc1, fc2 = din("fc1", (C, LAT)), din("fc2", (LAT, C))
    w1g, w2g = din("w1g", (2 * E, 128, 4096)), din("w2g", (2 * E, 128, 4096))
    shu, shd = din("shu", (C, SHI)), din("shd", (SHI, C))
    mixk, mixr = din("mixk", (128, NCC)), din("mixr", (128, NCC))
    l1Wk, l1Wr, l1Wv = din("l1Wk", (C, H1)), din("l1Wr", (C, H1)), din("l1Wv", (H1, C))
    td = din("td", (128, NHC))
    gam, bet = din("gam", (1, C)), din("bet", (1, C))
    f1w, f1b = din("f1w", (C, INNER)), din("f1b", (1, INNER))
    f2w, f2b = din("f2w", (INNER, C)), din("f2b", (1, C))

    out = nc.dram_tensor("out", [OWN, C], F32, kind="ExternalOutput").ap()

    def scr(name, shape, dtype=BF16):
        return nc.dram_tensor(name, list(shape), dtype, kind="Internal").ap()

    s_xnf = scr("s_xnf", (T, C))
    s_xne = scr("s_xne", (EXT, C))
    s_cat = scr("s_cat", (EXT, C))
    s_hexp = scr("s_hexp", (E, EINT, EXT))
    s_hsh = scr("s_hsh", (SHI, EXT))
    s_cw = scr("s_cw", (E, EXT), F32)
    s_ylat = scr("s_ylat", (EXT, LAT))
    s_x1 = scr("s_x1", (EXT, C), F32)
    s_xn1 = scr("s_xn1", (EXT, C))
    s_att1 = scr("s_att1", (OWN, C), F32)
    s_krp = scr("s_krp", (H1, OWN))
    s_qsc = scr("s_qsc", (EXT, 1), F32)
    s_xq1 = scr("s_xq1", (OWN, C))
    s_xq2 = scr("s_xq2", (OWN, INNER))
    s_sgn1 = scr("s_sgn1", (C, INNER))
    s_sgn2 = scr("s_sgn2", (INNER, C))

    ones_row_bf = nc.inline_tensor(np.ones((1, 128), dtype=BF), "ones_row")
    ident_f32 = nc.inline_tensor(np.eye(128, dtype=np.float32), "ident_f32")

    for s in ("sdma", "gdma", "ten", "vec", "scl", "gps", "tdma", "bar"):
        p.new_sem(s)

    sb = lambda name, shape, dtype=F32: es.enter_context(
        nc.sbuf_tensor(name, list(shape), dtype))

    NQT = EXT // 128 + (1 if EXT % 128 else 0)
    QTS = [(i * 128, min(128, EXT - i * 128)) for i in range(ceil_div(EXT, 128))]

    with es:
        c_bias = sb("c_bias", (128, NH * 44))
        c_ones = sb("c_ones", (1, 128), BF16)
        c_gateb = sb("c_gateb", (128, E))
        c_bq = sb("c_bq", (128, NCC))
        c_bk = sb("c_bk", (128, NCC))
        c_bv = sb("c_bv", (1, C), BF16)
        c_bo = sb("c_bo", (1, C), BF16)
        c_idf = sb("c_idf", (128, 128))
        cw_t = sb("cw_t", (128, ceil_div(EXT, 128) * E))
        st1 = sb("st1", (128, 1))
        st2 = sb("st2", (128, 1))
        st3 = sb("st3", (128, 1))
        st4 = sb("st4", (128, 8))
        st5 = sb("st5", (128, E))
        st6 = sb("st6", (128, E))
        st8 = sb("st8", (128, 8))
        st5g = sb("st5g", (128, E))
        qsc1 = sb("qsc1", (128, 9))
        qsc2 = sb("qsc2", (128, 8))
        c_td = sb("c_td", (128, NHC))
        c_w = sb("c_w", (128, NHC))
        c_mk = sb("c_mk", (128, NCC))
        c_mr = sb("c_mr", (128, NCC))
        c_sel = sb("c_sel", (128, 2))
        acc1 = sb("acc1", (128, 1))
        acc2 = sb("acc2", (128, 1))
        sw12 = sb("sw12", (128, 4))
        c_onesc = sb("c_onesc", (128, 1))
        c_onesr = sb("c_onesr", (1, 128))
        sc_dbg = sb("sc_dbg", (128, 9 * E)) if "scores" in DEBUG_TAPS else None
        lg_dbg = sb("lg_dbg", (128, 9 * E)) if "logits" in DEBUG_TAPS else None

        pb = [es.enter_context(nc.psum_tensor(f"pb{i}", [128, 512], F32))
              for i in range(8)]

        ring = WRing(p, "wr", 4, 4096)
        es_xnf = contextlib.ExitStack()
        xnf_fm = es_xnf.enter_context(nc.sbuf_tensor("xnf_fm", [128, NCC * T], BF16))
        es_xne = contextlib.ExitStack()
        xne_fm = es_xne.enter_context(nc.sbuf_tensor("xne_fm", [128, NCC * EXT], BF16))
        es_tmp = contextlib.ExitStack()
        tmp_a = es_tmp.enter_context(nc.sbuf_tensor("tmp_a", [128, C], F32))
        tmp_bf = es_tmp.enter_context(nc.sbuf_tensor("tmp_bf", [128, C], BF16))
        es_gate = contextlib.ExitStack()
        sbg = lambda name, shape, dtype=F32: es_gate.enter_context(
            nc.sbuf_tensor(name, list(shape), dtype))
        xne_lo = sbg("xne_lo", (128, NCC * EXT), BF16)
        es_attn = contextlib.ExitStack()

        p.dma("sync", c_bias[:], attn_bias, "sdma")
        p.dma("sync", c_ones[:], ones_row_bf.ap(), "sdma")
        p.dma("gpsimd", c_gateb[:], gate_b.broadcast_to([128, E]), "gdma")
        p.dma("sync", c_bq[:], bq, "sdma")
        p.dma("sync", c_bk[:], bk, "sdma")
        p.dma("gpsimd", c_bv[:], bv, "gdma")
        p.dma("gpsimd", c_bo[:], bo, "gdma")
        p.dma("sync", c_idf[:], ident_f32.ap(), "sdma")

        def ln0_tile(src_ap, dst_scr, t0, rows, want_lo, lo_scr):
            p.wait("sync", "vec")
            p.wait("sync", "tdma")
            p.wait("scalar", "tdma")
            p.dma("sync", tmp_a[0:rows, :], src_ap, "sdma")
            p.wait("vector", "sdma")
            p.wait("scalar", "sdma")
            p.run("vector", lambda e: e.reduce_sum(st1[0:rows], tmp_a[0:rows, :], axis=AX.X), "vec")
            p.run("scalar", lambda e: e.activation(tmp_bf[0:rows, :], tmp_a[0:rows, :], AF.Square,
                                                   accum_out=st2[0:rows]), "scl")
            p.wait("vector", "vec")
            p.wait("vector", "scl")
            p.run("vector", lambda e: e.tensor_single_scalar(st1[0:rows], st1[0:rows], 1.0 / C, ALU.mult), "vec")
            p.fence("vector", "vec")
            p.run("vector", lambda e: e.tensor_tensor(st3[0:rows], st1[0:rows], st1[0:rows], ALU.mult), "vec")
            p.run("vector", lambda e: e.tensor_scalar(st2[0:rows], st2[0:rows], 1.0 / C, 1e-5, ALU.mult, ALU.add), "vec")
            p.fence("vector", "vec")
            p.run("vector", lambda e: e.tensor_sub(st2[0:rows], st2[0:rows], st3[0:rows]), "vec")
            p.fence("vector", "vec")
            p.wait("scalar", "vec")
            p.run("scalar", lambda e: e.activation(st2[0:rows], st2[0:rows], AF.Sqrt), "scl")
            p.wait("vector", "scl")
            p.run("vector", lambda e: e.reciprocal(st2[0:rows], st2[0:rows]), "vec")
            p.fence("vector", "vec")
            p.run("vector", lambda e: e.tensor_scalar(tmp_a[0:rows, :], tmp_a[0:rows, :], st1[0:rows], None, ALU.subtract), "vec")
            p.fence("vector", "vec")
            p.run("vector", lambda e: e.tensor_scalar(tmp_a[0:rows, :], tmp_a[0:rows, :], st2[0:rows], None, ALU.mult), "vec")
            p.fence("vector", "vec")
            p.run("vector", lambda e: e.tensor_copy(tmp_bf[0:rows, :], tmp_a[0:rows, :]), "vec")
            p.fence("vector", "vec")
            p.wait("sync", "vec")
            p.dma("sync", dst_scr[t0:t0 + rows, :], tmp_bf[0:rows, :], "tdma")
            if want_lo:
                p.run("vector", lambda e: e.tensor_sub(tmp_a[0:rows, :], tmp_a[0:rows, :], tmp_bf[0:rows, :]), "vec")
                p.fence("vector", "vec")
                p.wait("vector", "tdma")
                p.run("vector", lambda e: e.tensor_copy(tmp_bf[0:rows, :], tmp_a[0:rows, :]), "vec")
                p.fence("vector", "vec")
                p.wait("sync", "vec")
                p.dma("sync", lo_scr[t0:t0 + rows, :], tmp_bf[0:rows, :], "tdma")

        s_xne_lo = scr("s_xne_lo", (EXT, C))
        for i in range(T // 128):
            ln0_tile(x_full[i * 128:(i + 1) * 128, :], s_xnf, i * 128, 128, False, None)
        for (t0, rows) in QTS:
            ln0_tile(x_ext[t0:t0 + rows, :], s_xne, t0, rows, True, s_xne_lo)

        p.wait("sync", "tdma")
        for cc in range(NCC):
            p.dma_tr("sync", xnf_fm[:, cc * T:(cc + 1) * T],
                     s_xnf[:, cc * 128:(cc + 1) * 128], "tdma")
            p.dma_tr("sync", xne_fm[:, cc * EXT:(cc + 1) * EXT],
                     s_xne[:, cc * 128:(cc + 1) * 128], "tdma")
            p.dma_tr("sync", xne_lo[:, cc * EXT:(cc + 1) * EXT],
                     s_xne_lo[:, cc * 128:(cc + 1) * 128], "tdma")

        gwf = sbg("gwf", (128, NCC * E))
        gw_hi = sbg("gw_hi", (128, NCC * E), BF16)
        gw_lo = sbg("gw_lo", (128, NCC * E), BF16)
        p.dma("sync", gwf[:].rearrange("p (cc e) -> p cc e", cc=NCC),
              gate_w.rearrange("(cc p) e -> p cc e", p=128), "sdma")
        p.wait("vector", "sdma")
        p.run("vector", lambda e: e.tensor_copy(gw_hi[:], gwf[:]), "vec")
        p.fence("vector", "vec")
        p.run("vector", lambda e: e.tensor_sub(gwf[:], gwf[:], gw_hi[:]), "vec")
        p.fence("vector", "vec")
        p.run("vector", lambda e: e.tensor_copy(gw_lo[:], gwf[:]), "vec")
        p.fence("vector", "vec")

        p.wait("tensor", "tdma")
        p.wait("tensor", "vec")
        for ti, (t0, rows) in enumerate(QTS):
            ps = pb[ti % 2]
            def mmgate(e, ti=ti, t0=t0, rows=rows, ps=ps):
                first = True
                last = None
                for cc in range(NCC):
                    for (xa, wa) in ((xne_fm, gw_hi), (xne_fm, gw_lo), (xne_lo, gw_hi)):
                        last = e.matmul(ps[0:rows, 0:E],
                                 xa[:, cc * EXT + t0: cc * EXT + t0 + rows],
                                 wa[:, cc * E:(cc + 1) * E],
                                 start=first, stop=(cc == NCC - 1 and wa is gw_hi and xa is xne_lo))
                        first = False
                return last
            if ti >= 2:
                p.wait("tensor", "scl")
            p.run("tensor", mmgate, "ten")
            p.wait("scalar", "ten")
            p.wait("scalar", "vec")
            if "logits" in DEBUG_TAPS:
                p.wait("vector", "ten")
                p.run("vector", lambda e, ti=ti, rows=rows, ps=ps: e.tensor_copy(
                    lg_dbg[0:rows, ti * E:(ti + 1) * E], ps[0:rows, 0:E]), "vec")
                p.fence("vector", "vec")
                p.wait("scalar", "vec")
            p.run("scalar", lambda e, ps=ps, rows=rows: e.activation(
                st5[0:rows], ps[0:rows, 0:E], AF.Sigmoid), "scl")
            p.wait("vector", "scl")
            p.wait("vector", "gdma")
            if "scores" in DEBUG_TAPS:
                p.run("vector", lambda e, ti=ti, rows=rows: e.tensor_copy(
                    sc_dbg[0:rows, ti * E:(ti + 1) * E], st5[0:rows]), "vec")
                p.fence("vector", "vec")
            p.run("vector", lambda e, rows=rows: e.tensor_add(st6[0:rows], st5[0:rows], c_gateb[0:rows]), "vec")
            p.fence("vector", "vec")
            for g in range(4):
                p.run("vector", lambda e, g=g, rows=rows: e.max(st8[0:rows], st6[0:rows, g * 8:(g + 1) * 8]), "vec")
                p.fence("vector", "vec")
                p.run("vector", lambda e, g=g, rows=rows: e.tensor_add(
                    st5g[0:rows, g * 8:(g + 1) * 8],
                    st8[0:rows, 0:1].broadcast_to([rows, 8]),
                    st8[0:rows, 1:2].broadcast_to([rows, 8])), "vec")
                p.fence("vector", "vec")
            p.run("vector", lambda e, rows=rows: e.memset(st4[0:rows, 4:8], -64.0), "vec")
            p.fence("vector", "vec")
            p.run("vector", lambda e, rows=rows: e.tensor_copy(
                st4[0:rows, 0:4], st5g[0:rows, 0:32:8]), "vec")
            p.fence("vector", "vec")
            p.run("vector", lambda e, rows=rows: e.max(st8[0:rows], st4[0:rows, 0:8]), "vec")
            p.fence("vector", "vec")
            p.run("vector", lambda e, rows=rows: e.tensor_scalar(
                st5g[0:rows], st5g[0:rows], st8[0:rows, 1:2], None, ALU.is_ge), "vec")
            p.fence("vector", "vec")
            p.run("vector", lambda e, rows=rows: e.scalar_tensor_tensor(
                st6[0:rows], st6[0:rows], 64.0, st5g[0:rows], ALU.add, ALU.mult), "vec")
            p.fence("vector", "vec")
            p.run("vector", lambda e, rows=rows: e.max(st8[0:rows], st6[0:rows]), "vec")
            p.fence("vector", "vec")
            p.run("vector", lambda e, rows=rows: e.tensor_scalar(
                st6[0:rows], st6[0:rows], st8[0:rows, 7:8], None, ALU.is_ge), "vec")
            p.fence("vector", "vec")
            p.run("vector", lambda e, rows=rows: e.tensor_mul(st6[0:rows], st6[0:rows], st5[0:rows]), "vec")
            p.fence("vector", "vec")
            p.run("vector", lambda e, rows=rows: e.reduce_sum(st1[0:rows], st6[0:rows], axis=AX.X), "vec")
            p.fence("vector", "vec")
            p.run("vector", lambda e, rows=rows: e.reciprocal(st2[0:rows], st1[0:rows]), "vec")
            p.fence("vector", "vec")
            p.run("vector", lambda e, rows=rows: e.tensor_scalar(
                st2[0:rows], st2[0:rows], 2.5, None, ALU.mult), "vec")
            p.fence("vector", "vec")
            p.run("vector", lambda e, ti=ti, rows=rows: e.tensor_scalar(
                cw_t[0:rows, ti * E:(ti + 1) * E], st6[0:rows], st2[0:rows], None, ALU.mult), "vec")
            p.fence("vector", "vec")
        p.barrier()
        es_gate.close()
        es_tmp.close()
        cstg = [es.enter_context(nc.sbuf_tensor(f"cstg{i}", [128, 128], BF16, side="right"))
                for i in range(2)]
        c_qoff = es.enter_context(nc.sbuf_tensor("c_qoff", [1, EXT], BF16, side="right"))
        q_all = es_attn.enter_context(nc.sbuf_tensor("q_all", [128, NH * EXT], BF16, side="right"))
        p.claim(q_all)
        RS = 0.08838834764831845
        p.wait("vector", "sdma")
        p.run("vector", lambda e: e.tensor_single_scalar(c_bq[:], c_bq[:], RS, ALU.mult), "vec")
        p.fence("vector", "vec")
        p.wait("scalar", "vec")
        EXT_SEGS = [(0, 512), (512, 512), (1024, 64)]
        for h in range(NH):
            gi = ring.n_issued
            ring.load(Wq.rearrange("(cc p) o -> p cc o", p=128)[:, :, h * 128:(h + 1) * 128],
                      cols=2048)
            grp = 0 if (h % 2 == 0) else 3
            if h >= 2:
                p.wait("tensor", "scl")
            ring.wait_granule("tensor", gi)

            def mmq(e, h=h, grp=grp, gbuf=ring.bufs[gi % ring.nslots]):
                last = None
                for cc in range(NCC):
                    for si, (s0, swd) in enumerate(EXT_SEGS):
                        last = e.matmul(pb[grp + si][0:128, 0:swd],
                                        gbuf[:, cc * 128:(cc + 1) * 128],
                                        xne_fm[:, cc * EXT + s0: cc * EXT + s0 + swd],
                                        start=(cc == 0), stop=(cc == NCC - 1))
                return last
            p.run("tensor", mmq, "ten")
            ring.done(1)
            p.wait("scalar", "ten")
            for si, (s0, swd) in enumerate(EXT_SEGS):
                p.run("scalar", lambda e, h=h, grp=grp, si=si, s0=s0, swd=swd: e.activation(
                    q_all[:, h * EXT + s0: h * EXT + s0 + swd],
                    pb[grp + si][0:128, 0:swd], AF.Identity,
                    bias=c_bq[:, h:h + 1], scale=RS), "scl")
        p.barrier()
        es_xne.close()
        c_mix0 = es_attn.enter_context(nc.sbuf_tensor("c_mix0", [128, 19 * 512], BF16, side="right"))
        p.claim(c_mix0)
        c_mix1 = es_attn.enter_context(nc.sbuf_tensor("c_mix1", [128, 2 * 64], BF16, side="right"))
        p.claim(c_mix1)
        p.dma("sync", c_mix0[:], mix0, "sdma")
        p.dma("sync", c_mix1[:], mix1, "sdma")
        p.wait("vector", "sdma")

        v4 = es_attn.enter_context(nc.sbuf_tensor("v4", [128, 16 * 516], BF16, side="right"))
        p.claim(v4)
        k_fm = es_attn.enter_context(nc.sbuf_tensor("k_fm", [128, T], BF16, side="right"))
        p.claim(k_fm)
        pT = es_attn.enter_context(nc.sbuf_tensor("pT", [128, 16 * 512], BF16, side="right"))
        p.claim(pT)
        for i in range(2):
            p.new_sem(f"catd{i}")
        n_cat = [0]
        QCOFF = [0, 12, 28]
        mix0_idx = {}
        mi = 0
        for qc in range(2):
            for kt in MIX[qc]:
                mix0_idx[(qc, kt)] = mi
                mi += 1
        mix1_idx = {(2, kt): i for i, kt in enumerate(MIX[2])}

        p.run("vector", lambda e: e.memset(
            v4[:].rearrange("p (m c) -> p m c", c=129)[:, :, 128:129], 1.0), "vec")
        p.fence("vector", "vec")

        for hg in range(4):
            gis = []
            for half in range(2):
                gis.append(ring.n_issued)
                ring.load(Wv.rearrange("(cc p) o -> p cc o", p=128)
                          [:, half * 8:(half + 1) * 8, hg * 512:(hg + 1) * 512])
            p.wait("tensor", "vec")
            for g in gis:
                ring.wait_granule("tensor", g)
            gb0 = ring.bufs[gis[0] % ring.nslots]
            gb1 = ring.bufs[gis[1] % ring.nslots]
            for tt in range(16):
                psv = pb[6 + (tt % 2)]
                if tt >= 2:
                    p.wait("tensor", "vec")

                def mmv(e, tt=tt, psv=psv, gb0=gb0, gb1=gb1, hg=hg):
                    for cc in range(NCC):
                        gb = gb0 if cc < 8 else gb1
                        e.matmul(psv[0:128, 0:512],
                                 xnf_fm[:, cc * T + tt * 128: cc * T + tt * 128 + 128],
                                 gb[:, (cc % 8) * 512:(cc % 8) * 512 + 512],
                                 start=(cc == 0), stop=False)
                    return e.matmul(psv[0:128, 0:512], c_ones[0:1, :],
                                    c_bv[0:1, hg * 512:(hg + 1) * 512],
                                    start=False, stop=True)
                p.run("tensor", mmv, "ten")
                if tt == 15:
                    ring.done(2)
                p.wait("vector", "ten")
                p.run("vector", lambda e, tt=tt, psv=psv: e.tensor_copy(
                    v4[:, tt * 516: tt * 516 + 516].rearrange(
                        "p (h c) -> p h c", h=4)[:, :, 0:128],
                    psv[0:128, 0:512].rearrange("p (h c) -> p h c", h=4)), "vec")
                p.fence("vector", "vec")
            for hh in range(4):
                h = hg * 4 + hh
                gi = ring.n_issued
                ring.load(Wk.rearrange("(cc p) o -> p cc o", p=128)[:, :, h * 128:(h + 1) * 128],
                          cols=2048)
                p.wait("tensor", "scl")
                p.wait("tensor", "vec")
                ring.wait_granule("tensor", gi)

                def mmk(e, gbuf=ring.bufs[gi % ring.nslots]):
                    last = None
                    for cc in range(NCC):
                        for seg in range(4):
                            last = e.matmul(pb[2 + seg][0:128, 0:512],
                                            gbuf[:, cc * 128:(cc + 1) * 128],
                                            xnf_fm[:, cc * T + seg * 512: cc * T + seg * 512 + 512],
                                            start=(cc == 0), stop=(cc == NCC - 1))
                    return last
                p.run("tensor", mmk, "ten")
                ring.done(1)
                p.wait("scalar", "ten")
                for seg in range(4):
                    p.run("scalar", lambda e, seg=seg, h=h: e.activation(
                        k_fm[:, seg * 512:(seg + 1) * 512], pb[2 + seg][0:128, 0:512],
                        AF.Identity, bias=c_bk[:, h:h + 1]), "scl")

                p.wait("sync", "ten")
                p.dma("sync", c_qoff[:], qoff[h:h + 1, :], "sdma")
                p.wait("tensor", "sdma")
                for qc, (qlo, qw, extn) in enumerate(QCS):
                    for kt in range(extn):
                        ps_s = pb[6 + (kt % 2)]
                        p.wait("tensor", "scl")
                        p.wait("tensor", "vec")

                        def mms(e, ps_s=ps_s, kt=kt, qlo=qlo, qw=qw, h=h):
                            e.matmul(ps_s[0:128, 0:qw], k_fm[:, kt * 128:(kt + 1) * 128],
                                     q_all[:, h * EXT + qlo: h * EXT + qlo + qw],
                                     start=True, stop=False)
                            return e.matmul(ps_s[0:128, 0:qw], c_ones[0:1, :],
                                            c_qoff[0:1, qlo:qlo + qw],
                                            start=False, stop=True)
                        p.run("tensor", mms, "ten")
                        in_mix = kt in MIX[qc]
                        if in_mix:
                            if qc < 2:
                                mi = mix0_idx[(qc, kt)]
                                msrc, moff = c_mix0, mi * 512
                            else:
                                mi = mix1_idx[(2, kt)]
                                msrc, moff = c_mix1, mi * 64
                            p.wait("vector", "ten")
                            p.run("vector", lambda e, ps_s=ps_s, qw=qw, msrc=msrc, moff=moff: e.tensor_add(
                                ps_s[0:128, 0:qw], ps_s[0:128, 0:qw],
                                msrc[:, moff: moff + qw]), "vec")
                            p.wait("scalar", "vec")
                        p.wait("scalar", "ten")
                        bcol = h * 44 + QCOFF[qc] + kt
                        p.run("scalar", lambda e, ps_s=ps_s, kt=kt, qw=qw, bcol=bcol: e.activation(
                            pT[:, kt * 512: kt * 512 + qw], ps_s[0:128, 0:qw],
                            AF.Exp, bias=c_bias[:, bcol:bcol + 1]), "scl")
                    p.wait("tensor", "scl")
                    p.wait("tensor", "vec")
                    nsub = ceil_div(qw, 128)
                    for sub in range(nsub):
                        sw = min(128, qw - sub * 128)
                        ps_o = pb[2 + (sub % 2)]
                        p.wait("tensor", "vec")

                        def mmpv(e, extn=extn, kt0=0, sub=sub, sw=sw, ps_o=ps_o, hh=hh):
                            last = None
                            for kt in range(extn):
                                last = e.matmul(
                                    ps_o[0:sw, 0:129],
                                    pT[:, kt * 512 + sub * 128: kt * 512 + sub * 128 + sw],
                                    v4[:, kt * 516 + hh * 129: kt * 516 + hh * 129 + 129],
                                    start=(kt == 0), stop=(kt == extn - 1))
                            return last
                        p.run("tensor", mmpv, "ten")
                        p.wait("vector", "ten")
                        p.run("vector", lambda e, ps_o=ps_o, sw=sw: e.reciprocal(
                            st1[0:sw], ps_o[0:sw, 128:129]), "vec")
                        p.fence("vector", "vec")
                        cslot = n_cat[0] % 2
                        if n_cat[0] >= 2:
                            p.wait("vector", f"catd{cslot}",
                                   16 * (n_cat[0] // 2))
                        p.run("vector", lambda e, ps_o=ps_o, sw=sw, cslot=cslot: e.tensor_scalar(
                            cstg[cslot][0:sw, :], ps_o[0:sw, 0:128], st1[0:sw], None,
                            ALU.mult), "vec")
                        p.fence("vector", "vec")
                        p.wait("sync", "vec")
                        r0 = qlo + sub * 128
                        p.dma("sync", s_cat[r0:r0 + sw, h * 128:(h + 1) * 128],
                              cstg[cslot][0:sw, :], f"catd{cslot}")
                        n_cat[0] += 1

        p.barrier()
        es_attn.close()
        es_xnf.close()

        es_p3 = contextlib.ExitStack()
        cat_fm = es_p3.enter_context(nc.sbuf_tensor("cat_fm", [128, NCC * EXT], BF16, side="right"))
        p.claim(cat_fm)
        es_x1 = contextlib.ExitStack()
        x1_acc = es_x1.enter_context(nc.sbuf_tensor("x1_acc", [128, len(QTS) * C], F32))
        for cc in range(NCC):
            p.dma_tr("sync", cat_fm[:, cc * EXT:(cc + 1) * EXT],
                     s_cat[:, cc * 128:(cc + 1) * 128], "tdma")
        p.run("vector", lambda e: e.memset(x1_acc[:], 0.0), "vec")
        p.fence("vector", "vec")
        p.wait("sync", "vec")
        for ti, (t0, rows) in enumerate(QTS):
            p.dma("sync", x1_acc[0:rows, ti * C:(ti + 1) * C], x_ext[t0:t0 + rows, :], "sdma")
        p.wait("tensor", "tdma")
        for ob in range(4):
            gis = []
            for half in range(2):
                gis.append(ring.n_issued)
                ring.load(Wo.rearrange("(cc p) o -> p cc o", p=128)
                          [:, half * 8:(half + 1) * 8, ob * 512:(ob + 1) * 512])
            for g in gis:
                ring.wait_granule("tensor", g)
            gb0 = ring.bufs[gis[0] % ring.nslots]
            gb1 = ring.bufs[gis[1] % ring.nslots]
            for ti, (t0, rows) in enumerate(QTS):
                pso = pb[ti % 2]
                p.wait("tensor", "vec")

                def mmo(e, t0=t0, rows=rows, pso=pso, ob=ob, gb0=gb0, gb1=gb1):
                    for cc in range(NCC):
                        gb = gb0 if cc < 8 else gb1
                        e.matmul(pso[0:rows, 0:512],
                                 cat_fm[:, cc * EXT + t0: cc * EXT + t0 + rows],
                                 gb[:, (cc % 8) * 512:(cc % 8) * 512 + 512],
                                 start=(cc == 0), stop=False)
                    return e.matmul(pso[0:rows, 0:512], c_ones[0:1, 0:rows],
                                    c_bo[0:1, ob * 512:(ob + 1) * 512],
                                    start=False, stop=True)
                p.run("tensor", mmo, "ten")
                if ti == len(QTS) - 1:
                    ring.done(2)
                p.wait("vector", "ten")
                p.wait("vector", "sdma")
                p.run("vector", lambda e, ti=ti, rows=rows, ob=ob, pso=pso: e.tensor_add(
                    x1_acc[0:rows, ti * C + ob * 512: ti * C + ob * 512 + 512],
                    x1_acc[0:rows, ti * C + ob * 512: ti * C + ob * 512 + 512],
                    pso[0:rows, 0:512]), "vec")
                p.fence("vector", "vec")
        p.barrier()
        es_p3.close()
        p.tap("x1a", x1_acc[:], (128, len(QTS) * C), F32)

        hstg = [es.enter_context(nc.sbuf_tensor(f"hstg{i}", [128, EXT], BF16, side="right"))
                for i in range(2)]
        hblk = es.enter_context(nc.sbuf_tensor("hblk", [128, 2 * 4096], BF16, side="right"))
        es_p4x = contextlib.ExitStack()
        x_lat = es_p4x.enter_context(nc.sbuf_tensor("x_lat", [128, 8 * EXT], BF16, side="right"))
        p.claim(x_lat)
        es_p4a = contextlib.ExitStack()
        xne2 = es_p4a.enter_context(nc.sbuf_tensor("xne2", [128, NCC * EXT], BF16, side="right"))
        p.claim(xne2)
        for cc in range(NCC):
            p.dma_tr("sync", xne2[:, cc * EXT:(cc + 1) * EXT],
                     s_xne[:, cc * 128:(cc + 1) * 128], "tdma")
        p.wait("tensor", "tdma")
        for lg in range(4):
            gi = ring.n_issued
            ring.load(fc1.rearrange("(cc p) o -> p cc o", p=128)
                      [:, :, lg * 256:(lg + 1) * 256], cols=4096)
            ring.wait_granule("tensor", gi)
            gbuf = ring.bufs[gi % ring.nslots]
            for lt in range(2):
                li = lg * 2 + lt
                grp = 0 if li % 2 == 0 else 3
                p.wait("tensor", "scl")

                def mmf1(e, lt=lt, grp=grp, gbuf=gbuf):
                    last = None
                    for cc in range(NCC):
                        for si, (s0, swd) in enumerate(EXT_SEGS):
                            last = e.matmul(pb[grp + si][0:128, 0:swd],
                                            gbuf[:, cc * 256 + lt * 128: cc * 256 + lt * 128 + 128],
                                            xne2[:, cc * EXT + s0: cc * EXT + s0 + swd],
                                            start=(cc == 0), stop=(cc == NCC - 1))
                    return last
                p.run("tensor", mmf1, "ten")
                p.wait("scalar", "ten")
                for si, (s0, swd) in enumerate(EXT_SEGS):
                    p.run("scalar", lambda e, li=li, grp=grp, si=si, s0=s0, swd=swd: e.activation(
                        x_lat[:, li * EXT + s0: li * EXT + s0 + swd],
                        pb[grp + si][0:128, 0:swd], AF.Copy), "scl")
            ring.done(1)

        for i in range(2):
            p.new_sem(f"hsd{i}")
        n_hst = [0]

        def spill_h_tile(src_fn, dst_ap, relu2=True, cwslot=None):
            pass

        for ig in range(16):
            gi = ring.n_issued
            ring.load(shu.rearrange("(cc p) o -> p cc o", p=128)
                      [:, :, ig * 256:(ig + 1) * 256], cols=4096)
            ring.wait_granule("tensor", gi)
            gbuf = ring.bufs[gi % ring.nslots]
            for it2 in range(2):
                it = ig * 2 + it2
                grp = 0 if it % 2 == 0 else 3
                p.wait("tensor", "scl")

                def mmsu(e, it2=it2, grp=grp, gbuf=gbuf):
                    last = None
                    for cc in range(NCC):
                        for si, (s0, swd) in enumerate(EXT_SEGS):
                            last = e.matmul(pb[grp + si][0:128, 0:swd],
                                            gbuf[:, cc * 256 + it2 * 128: cc * 256 + it2 * 128 + 128],
                                            xne2[:, cc * EXT + s0: cc * EXT + s0 + swd],
                                            start=(cc == 0), stop=(cc == NCC - 1))
                    return last
                p.run("tensor", mmsu, "ten")
                slot = n_hst[0] % 2
                if n_hst[0] >= 2:
                    p.wait("scalar", f"hsd{slot}", 16 * (n_hst[0] // 2))
                    p.wait("vector", f"hsd{slot}", 16 * (n_hst[0] // 2))
                p.wait("scalar", "ten")
                for si, (s0, swd) in enumerate(EXT_SEGS):
                    p.run("scalar", lambda e, grp=grp, si=si, s0=s0, swd=swd, slot=slot: e.activation(
                        hstg[slot][:, s0:s0 + swd], pb[grp + si][0:128, 0:swd],
                        AF.Relu), "scl")
                p.wait("vector", "scl")
                p.run("vector", lambda e, slot=slot: e.tensor_mul(
                    hstg[slot][:], hstg[slot][:], hstg[slot][:]), "vec")
                p.fence("vector", "vec")
                p.wait("sync", "vec")
                p.dma("sync", s_hsh[it * 128:(it + 1) * 128, :].rearrange("i t -> i t"),
                      hstg[slot][:], f"hsd{slot}")
                n_hst[0] += 1
            ring.done(1)
        p.barrier()
        es_p4a.close()

        p.new_sem("hbd")
        TPAIRS = [(0, 2), (2, 2), (4, 2), (6, 2), (8, 1)]
        for tpi, (tp0, ntt) in enumerate(TPAIRS):
            tw = sum(QTS[tp0 + j][1] for j in range(ntt))
            t0 = QTS[tp0][0]
            p.wait("sync", "ten")
            p.dma("sync", hblk[:].rearrange("p (ic t) -> p ic t", ic=32)[:, :, 0:tw],
                  s_hsh.rearrange("(ic p) t -> p ic t", p=128)[:, :, t0:t0 + tw], "hbd")
            p.wait("tensor", "hbd")
            for ob in range(4):
                gi = ring.n_issued
                ring.load(shd.rearrange("(ic p) o -> p ic o", p=128)
                          [:, 0:8, ob * 512:(ob + 1) * 512], cols=4096)
                gi2 = ring.n_issued
                ring.load(shd.rearrange("(ic p) o -> p ic o", p=128)
                          [:, 8:16, ob * 512:(ob + 1) * 512], cols=4096)
                gi3 = ring.n_issued
                ring.load(shd.rearrange("(ic p) o -> p ic o", p=128)
                          [:, 16:24, ob * 512:(ob + 1) * 512], cols=4096)
                gi4 = ring.n_issued
                ring.load(shd.rearrange("(ic p) o -> p ic o", p=128)
                          [:, 24:32, ob * 512:(ob + 1) * 512], cols=4096)
                for g in (gi, gi2, gi3, gi4):
                    ring.wait_granule("tensor", g)
                gbs = [ring.bufs[g % ring.nslots] for g in (gi, gi2, gi3, gi4)]
                for j in range(ntt):
                    ti = tp0 + j
                    rows = QTS[ti][1]
                    pso = pb[(ob % 2) * 4 + j * 2]
                    pso = pb[2 * j + (ob % 2) * 4]

                    def mmsd(e, j=j, rows=rows, pso=pso, gbs=gbs, tw=tw):
                        last = None
                        for ic in range(32):
                            gb = gbs[ic // 8]
                            last = e.matmul(
                                pso[0:rows, 0:512],
                                hblk[:].rearrange("p (ic t) -> p ic t", ic=32)
                                [:, ic, j * 128: j * 128 + rows],
                                gb[:, (ic % 8) * 512:(ic % 8) * 512 + 512],
                                start=(ic == 0), stop=(ic == 31))
                        return last
                    p.wait("tensor", "vec")
                    p.run("tensor", mmsd, "ten")
                    p.wait("vector", "ten")
                    p.run("vector", lambda e, ti=ti, rows=rows, ob=ob, pso=pso: e.tensor_add(
                        x1_acc[0:rows, ti * C + ob * 512: ti * C + ob * 512 + 512],
                        x1_acc[0:rows, ti * C + ob * 512: ti * C + ob * 512 + 512],
                        pso[0:rows, 0:512]), "vec")
                    p.fence("vector", "vec")
                ring.done(4)
        p.barrier()

        es_p4c = contextlib.ExitStack()
        cwTs = es_p4c.enter_context(nc.sbuf_tensor("cwTs", [E, EXT], F32, side="right"))
        p.claim(cwTs)
        p.wait("tensor", "sdma")
        p.wait("tensor", "vec")
        for ti, (t0, rows) in enumerate(QTS):
            p.wait("tensor", "vec")
            p.run("tensor", lambda e, ti=ti, rows=rows: e.transpose(
                pb[ti % 2][0:E, 0:rows].rearrange("e r -> e r"),
                cw_t[0:rows, ti * E:(ti + 1) * E], c_idf[0:rows, 0:rows]), "ten")
            p.wait("vector", "ten")
            p.run("vector", lambda e, ti=ti, t0=t0, rows=rows: e.tensor_copy(
                cwTs[:, t0:t0 + rows], pb[ti % 2][0:E, 0:rows]), "vec")
            p.fence("vector", "vec")
        p.wait("sync", "vec")
        p.dma("sync", s_cw, cwTs[:], "sdma")
        p.barrier()

        cwb = [es_p4c.enter_context(nc.sbuf_tensor(f"cwb{i}", [128, EXT], BF16, side="right"))
               for i in range(2)]
        for _t in cwb:
            p.claim(_t)
        p.new_sem("cwd0")
        p.new_sem("cwd1")
        for e_i in range(E):
            cslot = e_i % 2
            if e_i >= 2:
                p.wait("gpsimd", "vec")
            p.dma("gpsimd", cwb[cslot][:], s_cw[e_i:e_i + 1, :].broadcast_to([128, EXT]),
                  f"cwd{cslot}")
            for half in range(2):
                gi = ring.n_issued
                ring.load(w1g[e_i * 2 + half], cols=4096)
                ring.wait_granule("tensor", gi)
                gbuf = ring.bufs[gi % ring.nslots]
                for it2 in range(4):
                    it = half * 4 + it2
                    grp = 0 if it % 2 == 0 else 3
                    p.wait("tensor", "scl")

                    def mme1(e, it2=it2, grp=grp, gbuf=gbuf):
                        last = None
                        for lc in range(8):
                            for si, (s0, swd) in enumerate(EXT_SEGS):
                                last = e.matmul(pb[grp + si][0:128, 0:swd],
                                                gbuf[:, lc * 512 + it2 * 128: lc * 512 + it2 * 128 + 128],
                                                x_lat[:, lc * EXT + s0: lc * EXT + s0 + swd],
                                                start=(lc == 0), stop=(lc == 7))
                        return last
                    p.run("tensor", mme1, "ten")
                    slot = n_hst[0] % 2
                    if n_hst[0] >= 2:
                        p.wait("scalar", f"hsd{slot}", 16 * (n_hst[0] // 2))
                        p.wait("vector", f"hsd{slot}", 16 * (n_hst[0] // 2))
                    p.wait("scalar", "ten")
                    for si, (s0, swd) in enumerate(EXT_SEGS):
                        p.run("scalar", lambda e, grp=grp, si=si, s0=s0, swd=swd, slot=slot: e.activation(
                            hstg[slot][:, s0:s0 + swd], pb[grp + si][0:128, 0:swd],
                            AF.Relu), "scl")
                    p.wait("vector", "scl")
                    p.wait("vector", f"cwd{cslot}", 16 * (e_i // 2 + 1))
                    p.run("vector", lambda e, slot=slot: e.tensor_mul(
                        hstg[slot][:], hstg[slot][:], hstg[slot][:]), "vec")
                    p.fence("vector", "vec")
                    p.run("vector", lambda e, slot=slot, cslot=cslot: e.tensor_mul(
                        hstg[slot][:], hstg[slot][:], cwb[cslot][:]), "vec")
                    p.fence("vector", "vec")
                    p.wait("sync", "vec")
                    p.dma("sync", s_hexp[e_i, it * 128:(it + 1) * 128, :],
                          hstg[slot][:], f"hsd{slot}")
                    n_hst[0] += 1
                ring.done(1)
        p.barrier()
        es_p4c.close()
        es_p4x.close()

        es_p4y = contextlib.ExitStack()
        ylat_bf = es_p4y.enter_context(nc.sbuf_tensor("ylat_bf", [128, len(QTS) * LAT], BF16, side="right"))
        p.claim(ylat_bf)
        EPASS = [(0, 4, 512), (4, 4, 512), (8, 1, 64)]
        for (tp0, ntt, tw) in EPASS:
            t0 = QTS[tp0][0]
            for e_i in range(E):
                gi = ring.n_issued
                ring.load(s_hexp.rearrange("e (ic p) t -> e p ic t", p=128)
                          [e_i][:, :, t0:t0 + tw], cols=8 * tw)
                gw0 = ring.n_issued
                ring.load(w2g[e_i * 2 + 0], cols=4096)
                gw1 = ring.n_issued
                ring.load(w2g[e_i * 2 + 1], cols=4096)
                for g in (gi, gw0, gw1):
                    ring.wait_granule("tensor", g)
                hb = ring.bufs[gi % ring.nslots]
                wb = [ring.bufs[gw0 % ring.nslots], ring.bufs[gw1 % ring.nslots]]

                def mme2(e, e_i=e_i, ntt=ntt, tw=tw, tp0=tp0, hb=hb, wb=wb):
                    last = None
                    for j in range(ntt):
                        rows = QTS[tp0 + j][1]
                        for ic in range(8):
                            for lh in range(2):
                                last = e.matmul(
                                    pb[2 * j + lh][0:rows, 0:512],
                                    hb[:, ic * tw + j * 128: ic * tw + j * 128 + rows],
                                    wb[lh][:, ic * 512:(ic + 1) * 512],
                                    start=(ic == 0 and e_i == 0),
                                    stop=(ic == 7 and e_i == E - 1))
                    return last
                p.run("tensor", mme2, "ten")
                ring.done(3)
            p.wait("scalar", "ten")
            for j in range(ntt):
                ti = tp0 + j
                rows = QTS[ti][1]
                for lh in range(2):
                    p.run("scalar", lambda e, ti=ti, j=j, lh=lh, rows=rows: e.activation(
                        ylat_bf[0:rows, ti * LAT + lh * 512: ti * LAT + lh * 512 + 512],
                        pb[2 * j + lh][0:rows, 0:512], AF.Copy), "scl")
            p.wait("tensor", "scl")
        p.wait("sync", "scl")
        for ti, (t0, rows) in enumerate(QTS):
            p.dma("sync", s_ylat[t0:t0 + rows, :], ylat_bf[0:rows, ti * LAT:(ti + 1) * LAT], "tdma")
        p.barrier()
        ylat_fm = es_p4y.enter_context(nc.sbuf_tensor("ylat_fm", [128, 8 * EXT], BF16, side="right"))
        p.claim(ylat_fm)
        for lc in range(8):
            p.dma_tr("sync", ylat_fm[:, lc * EXT:(lc + 1) * EXT],
                     s_ylat[:, lc * 128:(lc + 1) * 128], "tdma")
        p.wait("tensor", "tdma")
        for ob in range(4):
            gi = ring.n_issued
            ring.load(fc2.rearrange("(lc p) o -> p lc o", p=128)
                      [:, :, ob * 512:(ob + 1) * 512], cols=4096)
            ring.wait_granule("tensor", gi)
            gbuf = ring.bufs[gi % ring.nslots]
            for ti, (t0, rows) in enumerate(QTS):
                pso = pb[ti % 2]
                p.wait("tensor", "vec")

                def mmf2(e, t0=t0, rows=rows, pso=pso, gbuf=gbuf):
                    last = None
                    for lc in range(8):
                        last = e.matmul(pso[0:rows, 0:512],
                                        ylat_fm[:, lc * EXT + t0: lc * EXT + t0 + rows],
                                        gbuf[:, lc * 512:(lc + 1) * 512],
                                        start=(lc == 0), stop=(lc == 7))
                    return last
                p.run("tensor", mmf2, "ten")
                if ti == len(QTS) - 1:
                    ring.done(1)
                p.wait("vector", "ten")
                p.run("vector", lambda e, ti=ti, rows=rows, ob=ob, pso=pso: e.tensor_add(
                    x1_acc[0:rows, ti * C + ob * 512: ti * C + ob * 512 + 512],
                    x1_acc[0:rows, ti * C + ob * 512: ti * C + ob * 512 + 512],
                    pso[0:rows, 0:512]), "vec")
                p.fence("vector", "vec")
        p.barrier()
        es_p4y.close()
        p.tap("x1", x1_acc[:], (128, len(QTS) * C), F32)
        for ti, (t0, rows) in enumerate(QTS):
            p.dma("sync", s_x1[t0:t0 + rows, :], x1_acc[0:rows, ti * C:(ti + 1) * C], "tdma")
        p.wait("sync", "tdma")

        es_p5t = contextlib.ExitStack()
        t5a = es_p5t.enter_context(nc.sbuf_tensor("t5a", [128, C], F32, side="right"))
        p.claim(t5a)
        t5b = es_p5t.enter_context(nc.sbuf_tensor("t5b", [128, C], F32, side="right"))
        p.claim(t5b)
        t5i = es_p5t.enter_context(nc.sbuf_tensor("t5i", [128, C], I32, side="right"))
        p.claim(t5i)
        t5bf = es_p5t.enter_context(nc.sbuf_tensor("t5bf", [128, C], BF16, side="right"))
        p.claim(t5bf)
        for ti, (t0, rows) in enumerate(QTS):
            xa = x1_acc[0:rows, ti * C:(ti + 1) * C]
            p.wait("vector", "tdma")
            p.wait("scalar", "tdma")
            p.run("vector", lambda e, xa=xa, rows=rows: e.reduce_sum(st1[0:rows], xa, axis=AX.X), "vec")
            p.run("scalar", lambda e, xa=xa, rows=rows: e.activation(t5bf[0:rows, :], xa, AF.Square,
                                                                     accum_out=st2[0:rows]), "scl")
            p.wait("vector", "vec")
            p.wait("vector", "scl")
            p.run("vector", lambda e, rows=rows: e.tensor_single_scalar(st1[0:rows], st1[0:rows], 1.0 / C, ALU.mult), "vec")
            p.fence("vector", "vec")
            p.run("vector", lambda e, rows=rows: e.tensor_tensor(st3[0:rows], st1[0:rows], st1[0:rows], ALU.mult), "vec")
            p.run("vector", lambda e, rows=rows: e.tensor_scalar(st2[0:rows], st2[0:rows], 1.0 / C, 1e-5, ALU.mult, ALU.add), "vec")
            p.fence("vector", "vec")
            p.run("vector", lambda e, rows=rows: e.tensor_sub(st2[0:rows], st2[0:rows], st3[0:rows]), "vec")
            p.fence("vector", "vec")
            p.wait("scalar", "vec")
            p.run("scalar", lambda e, rows=rows: e.activation(st2[0:rows], st2[0:rows], AF.Sqrt), "scl")
            p.wait("vector", "scl")
            p.run("vector", lambda e, rows=rows: e.reciprocal(st2[0:rows], st2[0:rows]), "vec")
            p.fence("vector", "vec")
            p.run("vector", lambda e, xa=xa, rows=rows: e.tensor_scalar(t5a[0:rows, :], xa, st1[0:rows], None, ALU.subtract), "vec")
            p.fence("vector", "vec")
            p.run("vector", lambda e, rows=rows: e.tensor_scalar(t5a[0:rows, :], t5a[0:rows, :], st2[0:rows], None, ALU.mult), "vec")
            p.fence("vector", "vec")
            p.run("vector", lambda e, rows=rows: e.tensor_copy(t5bf[0:rows, :], t5a[0:rows, :]), "vec")
            p.fence("vector", "vec")
            p.wait("sync", "vec")
            p.dma("sync", s_xn1[t0:t0 + rows, :], t5bf[0:rows, :], "tdma")
            p.wait("scalar", "tdma")
            p.run("scalar", lambda e, rows=rows: e.activation(t5bf[0:rows, :], t5a[0:rows, :], AF.Square,
                                                              accum_out=st2[0:rows]), "scl")
            p.run("vector", lambda e, rows=rows: e.reduce_max(st3[0:rows], t5a[0:rows, :], axis=AX.X,
                                                              apply_absolute_value=True), "vec")
            p.wait("vector", "scl")
            p.fence("vector", "vec")
            p.run("vector", lambda e, rows=rows: e.tensor_scalar(st2[0:rows], st2[0:rows], 1.0 / C, 1e-6, ALU.mult, ALU.add), "vec")
            p.fence("vector", "vec")
            p.wait("scalar", "vec")
            p.run("scalar", lambda e, rows=rows: e.activation(st2[0:rows], st2[0:rows], AF.Sqrt), "scl")
            p.wait("vector", "scl")
            p.run("vector", lambda e, rows=rows: e.reciprocal(st2[0:rows], st2[0:rows]), "vec")
            p.fence("vector", "vec")
            p.run("vector", lambda e, rows=rows: e.tensor_single_scalar(st2[0:rows], st2[0:rows], C ** -0.5, ALU.mult), "vec")
            p.fence("vector", "vec")
            p.run("vector", lambda e, rows=rows: e.tensor_tensor(st3[0:rows], st3[0:rows], st2[0:rows], ALU.mult), "vec")
            p.fence("vector", "vec")
            p.run("vector", lambda e, rows=rows: e.tensor_scalar_max(st3[0:rows], st3[0:rows], 1e-5), "vec")
            p.fence("vector", "vec")
            p.run("vector", lambda e, rows=rows: e.reciprocal(st3[0:rows], st3[0:rows]), "vec")
            p.fence("vector", "vec")
            p.run("vector", lambda e, rows=rows: e.tensor_single_scalar(st3[0:rows], st3[0:rows], 127.0, ALU.mult), "vec")
            p.fence("vector", "vec")
            p.run("vector", lambda e, ti=ti, rows=rows: e.tensor_copy(qsc1[0:rows, ti:ti + 1], st3[0:rows]), "vec")
            p.fence("vector", "vec")
            p.wait("sync", "vec")
            p.dma("sync", s_qsc[t0:t0 + rows, :], st3[0:rows], "tdma")
            p.wait("vector", "tdma")
            p.run("vector", lambda e, rows=rows: e.tensor_tensor(st3[0:rows], st3[0:rows], st2[0:rows], ALU.mult), "vec")
            p.fence("vector", "vec")
            p.run("vector", lambda e, rows=rows: e.tensor_scalar(t5a[0:rows, :], t5a[0:rows, :], st3[0:rows], None, ALU.mult), "vec")
            p.fence("vector", "vec")
            if ti == 0:
                p.wait("sync", "vec")
                p.tap("z0", t5a[:], (128, C), F32)
                p.tap("m0", st3[:], (128, 1), F32)
            p.run("vector", lambda e, rows=rows: e.tensor_copy(t5i[0:rows, :], t5a[0:rows, :]), "vec")
            p.fence("vector", "vec")
            p.run("vector", lambda e, rows=rows: e.tensor_copy(t5b[0:rows, :], t5i[0:rows, :]), "vec")
            p.fence("vector", "vec")
            p.run("vector", lambda e, rows=rows: e.tensor_sub(t5a[0:rows, :], t5a[0:rows, :], t5b[0:rows, :]), "vec")
            p.fence("vector", "vec")
            p.run("vector", lambda e, rows=rows: e.scalar_tensor_tensor(t5b[0:rows, :], t5a[0:rows, :], 0.5, t5b[0:rows, :], ALU.is_ge, ALU.add), "vec")
            p.fence("vector", "vec")
            p.run("vector", lambda e, rows=rows: e.tensor_single_scalar(t5a[0:rows, :], t5a[0:rows, :], -0.5, ALU.is_le), "vec")
            p.fence("vector", "vec")
            p.run("vector", lambda e, rows=rows: e.tensor_tensor(t5bf[0:rows, :], t5b[0:rows, :], t5a[0:rows, :], ALU.subtract), "vec")
            p.fence("vector", "vec")
            p.wait("sync", "vec")
            p.dma("sync", s_xq1[max(0, t0 - LB):max(0, t0 - LB) + rows - (LB if ti == 0 else 0), :],
                  t5bf[(LB if ti == 0 else 0):rows, :], "tdma")
        p.barrier()
        es_p5t.close()
        es_x1.close()

        es_p5x = contextlib.ExitStack()
        gateb = es_p5x.enter_context(nc.sbuf_tensor("gateb", [128, EXT], F32, side="right"))
        p.claim(gateb)
        p.dma("sync", c_td[:], td, "sdma")
        p.dma("sync", c_mk[:], mixk, "sdma")
        p.dma("sync", c_mr[:], mixr, "sdma")
        p.dma("gpsimd", c_sel[:], sel_prev.broadcast_to([128, 2]), "gdma")
        p.dma("gpsimd", gateb[:], scan_gate.broadcast_to([128, EXT]), "gdma")
        p.wait("scalar", "sdma")
        p.run("scalar", lambda e: e.activation(c_w[:], c_td[:], AF.Exp), "scl")
        p.fence("scalar", "scl")
        p.run("scalar", lambda e: e.activation(c_w[:], c_w[:], AF.Exp, scale=-1.0), "scl")
        p.wait("vector", "scl")

        xk = es_p5x.enter_context(nc.sbuf_tensor("xk", [128, NCC * EXT], BF16, side="right"))
        p.claim(xk)
        xr = es_p5x.enter_context(nc.sbuf_tensor("xr", [128, NCC * OWN], BF16, side="right"))
        p.claim(xr)
        xn1c = [es_p5x.enter_context(nc.sbuf_tensor(f"xn1c{i}", [128, EXT], BF16, side="right"))
                for i in range(2)]
        for _t in xn1c:
            p.claim(_t)
        t5junk = es_p5x.enter_context(nc.sbuf_tensor("t5junk", [128, EXT], BF16, side="right"))
        p.claim(t5junk)
        for i in range(2):
            p.new_sem(f"xnd{i}")
        p.wait("vector", "gdma")
        for cc in range(NCC):
            cslot2 = cc % 2
            if cc >= 2:
                p.wait("sync", "vec")
            p.dma_tr("sync", xn1c[cslot2][:], s_xn1[:, cc * 128:(cc + 1) * 128],
                     f"xnd{cslot2}")
            p.wait("vector", f"xnd{cslot2}", 16 * (cc // 2 + 1))
            xn1f = xn1c[cslot2]
            base = 0
            for (dst, dof, mixc) in ((xk, cc * EXT, c_mk), (xr, cc * OWN, c_mr)):
                own0 = 0 if dst is xk else LB
                width = EXT if dst is xk else OWN
                p.run("vector", lambda e, xn1f=xn1f, mixc=mixc, cc=cc: e.tensor_scalar(
                    st1[:], mixc[:, cc:cc + 1], -1.0, 1.0, ALU.mult, ALU.add), "vec")
                p.fence("vector", "vec")
                p.run("vector", lambda e, xn1f=xn1f, dst=dst, dof=dof, base=base, own0=own0, width=width, cc=cc: e.tensor_scalar(
                    dst[:, dof:dof + width], xn1f[:, base + own0: base + own0 + width],
                    st1[:], None, ALU.mult), "vec")
                p.fence("vector", "vec")
                sh0 = base + own0 - 1
                o0 = 0
                if own0 == 0:
                    sh0, o0 = base, 1
                w_eff = width - o0
                p.run("vector", lambda e, xn1f=xn1f, dst=dst, dof=dof, sh0=sh0, o0=o0, w_eff=w_eff, mixc=mixc, cc=cc: e.scalar_tensor_tensor(
                    t5junk[:, o0:o0 + w_eff], xn1f[:, sh0: sh0 + w_eff],
                    mixc[:, cc:cc + 1], dst[:, dof + o0:dof + o0 + w_eff], ALU.mult, ALU.add), "vec")
                p.fence("vector", "vec")
                p.run("vector", lambda e, dst=dst, dof=dof, o0=o0, w_eff=w_eff: e.tensor_copy(
                    dst[:, dof + o0:dof + o0 + w_eff], t5junk[:, o0:o0 + w_eff]), "vec")
                p.fence("vector", "vec")
                fix = LB - own0
                p.run("vector", lambda e, xn1f=xn1f, cc=cc, base=base: e.tensor_mul(
                    st1[:], xn1f[:, base + LB - 1: base + LB], c_sel[:, 0:1]), "vec")
                p.fence("vector", "vec")
                p.run("vector", lambda e, xn1f=xn1f, cc=cc, base=base: e.scalar_tensor_tensor(
                    st1[:], xn1f[:, base + LB + 1: base + LB + 2], c_sel[:, 1:2],
                    st1[:], ALU.mult, ALU.add), "vec")
                p.fence("vector", "vec")
                p.run("vector", lambda e, xn1f=xn1f, cc=cc, base=base: e.tensor_sub(
                    st1[:], st1[:], xn1f[:, base + LB: base + LB + 1]), "vec")
                p.fence("vector", "vec")
                p.run("vector", lambda e, xn1f=xn1f, mixc=mixc, cc=cc: e.tensor_scalar(
                    st1[:], st1[:], mixc[:, cc:cc + 1], None, ALU.mult), "vec")
                p.fence("vector", "vec")
                p.run("vector", lambda e, xn1f=xn1f, dst=dst, dof=dof, fix=fix, base=base, cc=cc: e.tensor_add(
                    dst[:, dof + fix: dof + fix + 1], st1[:],
                    xn1f[:, base + LB: base + LB + 1]), "vec")
                p.fence("vector", "vec")
                if dst is xk:
                    p.run("vector", lambda e, xn1f=xn1f, dst=dst, dof=dof, base=base: e.tensor_copy(
                        dst[:, dof: dof + 1], xn1f[:, base: base + 1]), "vec")
                    p.fence("vector", "vec")
        p.barrier()
        p.tap("xk", xk[:], (128, NCC * EXT), BF16)
        p.tap("xr", xr[:], (128, NCC * OWN), BF16)

        ku = [es_p5x.enter_context(nc.sbuf_tensor(f"ku{i}", [128, EXT], F32, side="right"))
              for i in range(2)]
        for _t in ku:
            p.claim(_t)
        ubuf = es_p5x.enter_context(nc.sbuf_tensor("ubuf", [128, EXT], F32, side="right"))
        p.claim(ubuf)
        rbuf = es_p5x.enter_context(nc.sbuf_tensor("rbuf", [128, OWN], F32, side="right"))
        p.claim(rbuf)
        wgate = es_p5x.enter_context(nc.sbuf_tensor("wgate", [128, EXT], F32, side="right"))
        p.claim(wgate)
        kstg = [es_p5x.enter_context(nc.sbuf_tensor(f"kstg{i}", [128, OWN], BF16, side="right"))
                for i in range(2)]
        for _t in kstg:
            p.claim(_t)
        for i in range(2):
            p.new_sem(f"kpd{i}")
        n_kp = [0]
        OWN_SEGS = [(0, 512), (512, 512)]
        for hp in range(11):
            gk = ring.n_issued
            ring.load(l1Wk.rearrange("(cc p) o -> p cc o", p=128)
                      [:, :, hp * 256:(hp + 1) * 256], cols=4096)
            gr = ring.n_issued
            ring.load(l1Wr.rearrange("(cc p) o -> p cc o", p=128)
                      [:, :, hp * 256:(hp + 1) * 256], cols=4096)
            ring.wait_granule("tensor", gk)
            gkb = ring.bufs[gk % ring.nslots]
            grb = ring.bufs[gr % ring.nslots]
            for j in range(2):
                grp = 3 * j
                p.wait("tensor", "scl")

                def mmk1(e, j=j, grp=grp, gkb=gkb):
                    last = None
                    for cc in range(NCC):
                        for si, (s0, swd) in enumerate(EXT_SEGS):
                            last = e.matmul(pb[grp + si][0:128, 0:swd],
                                            gkb[:, cc * 256 + j * 128: cc * 256 + j * 128 + 128],
                                            xk[:, cc * EXT + s0: cc * EXT + s0 + swd],
                                            start=(cc == 0), stop=(cc == NCC - 1))
                    return last
                p.run("tensor", mmk1, "ten")
                p.wait("scalar", "ten")
                p.wait("scalar", "vec")
                for si, (s0, swd) in enumerate(EXT_SEGS):
                    p.run("scalar", lambda e, j=j, grp=grp, si=si, s0=s0, swd=swd: e.activation(
                        ku[j][:, s0:s0 + swd], pb[grp + si][0:128, 0:swd], AF.Relu), "scl")
            ring.done(1)
            for j in range(2):
                hc = hp * 2 + j
                ring.wait_granule("tensor", gr)
                p.wait("tensor", "scl")
                p.wait("tensor", "vec")

                def mmr1(e, j=j, grb=grb):
                    last = None
                    for cc in range(NCC):
                        for si, (s0, swd) in enumerate(OWN_SEGS):
                            last = e.matmul(pb[6 + si][0:128, 0:swd],
                                            grb[:, cc * 256 + j * 128: cc * 256 + j * 128 + 128],
                                            xr[:, cc * OWN + s0: cc * OWN + s0 + swd],
                                            start=(cc == 0), stop=(cc == NCC - 1))
                    return last
                p.run("tensor", mmr1, "ten")
                if j == 1:
                    ring.done(1)
                p.wait("scalar", "ten")
                p.wait("scalar", "vec")
                for si, (s0, swd) in enumerate(OWN_SEGS):
                    p.run("scalar", lambda e, si=si, s0=s0, swd=swd: e.activation(
                        rbuf[:, s0:s0 + swd], pb[6 + si][0:128, 0:swd], AF.Copy), "scl")
                p.wait("vector", "scl")
                p.run("vector", lambda e, hc=hc: e.tensor_scalar(
                    wgate[:], gateb[:], c_w[:, hc:hc + 1], None, ALU.mult), "vec")
                p.fence("vector", "vec")
                if hp == 0 and j == 0:
                    p.wait("sync", "scl")
                    p.wait("sync", "vec")
                    p.tap("ku0", ku[0][:], (128, EXT), F32)
                    p.tap("rb0", rbuf[:], (128, OWN), F32)
                    p.tap("wg0", wgate[:], (128, EXT), F32)
                p.run("vector", lambda e, j=j: e.tensor_tensor_scan(
                    ubuf[:], wgate[:], ku[j][:], 0.0, ALU.mult, ALU.add), "vec")
                p.fence("vector", "vec")
                if hp == 0 and j == 0:
                    p.wait("sync", "vec")
                    p.tap("ub0", ubuf[:], (128, EXT), F32)
                kslot = n_kp[0] % 2
                if n_kp[0] >= 2:
                    p.wait("vector", f"kpd{kslot}", 16 * (n_kp[0] // 2))
                p.run("vector", lambda e, kslot=kslot: e.tensor_mul(
                    kstg[kslot][:], ubuf[:, LB:], rbuf[:]), "vec")
                p.fence("vector", "vec")
                p.wait("sync", "vec")
                p.dma("sync", s_krp[hc * 128:(hc + 1) * 128, :], kstg[kslot][:],
                      f"kpd{kslot}")
                n_kp[0] += 1
        p.barrier()
        es_p5x.close()

        es_p5o = contextlib.ExitStack()
        att1 = es_p5o.enter_context(nc.sbuf_tensor("att1", [128, 8 * C], F32, side="right"))
        p.claim(att1)
        jnk5 = es_p5o.enter_context(nc.sbuf_tensor("jnk5", [128, C], BF16, side="right"))
        p.claim(jnk5)
        gamb = es_p5o.enter_context(nc.sbuf_tensor("gamb", [128, C], F32, side="right"))
        p.claim(gamb)
        betb = es_p5o.enter_context(nc.sbuf_tensor("betb", [128, C], F32, side="right"))
        p.claim(betb)
        p.dma("gpsimd", gamb[:], gam.broadcast_to([128, C]), "gdma")
        p.dma("gpsimd", betb[:], bet.broadcast_to([128, C]), "gdma")
        HCGS = [(0, 4), (4, 4), (8, 4), (12, 4), (16, 4), (20, 2)]
        for ob in range(4):
            for gidx, (h0, nhc_g) in enumerate(HCGS):
                gk2 = ring.n_issued
                ring.load(s_krp.rearrange("(hc p) t -> p hc t", p=128)
                          [:, h0:h0 + nhc_g, :], cols=nhc_g * OWN)
                gi = ring.n_issued
                ring.load(l1Wv.rearrange("(hc p) o -> p hc o", p=128)
                          [:, h0:h0 + nhc_g, ob * 512:(ob + 1) * 512], cols=nhc_g * 512)
                ring.wait_granule("tensor", gk2)
                ring.wait_granule("tensor", gi)
                kbuf = ring.bufs[gk2 % ring.nslots]
                gbuf = ring.bufs[gi % ring.nslots]
                for ti in range(8):
                    if gidx == 0:
                        p.wait("tensor", "vec")

                    def mmov(e, ti=ti, nhc_g=nhc_g, kbuf=kbuf, gbuf=gbuf, gidx=gidx):
                        last = None
                        for hj in range(nhc_g):
                            last = e.matmul(pb[ti][0:128, 0:512],
                                            kbuf[:, hj * OWN + ti * 128: hj * OWN + ti * 128 + 128],
                                            gbuf[:, hj * 512:(hj + 1) * 512],
                                            start=(gidx == 0 and hj == 0),
                                            stop=(gidx == len(HCGS) - 1 and hj == nhc_g - 1))
                        return last
                    p.run("tensor", mmov, "ten")
                ring.done(2)
            p.wait("vector", "ten")
            for ti in range(8):
                p.run("vector", lambda e, ti=ti, ob=ob: e.tensor_copy(
                    att1[:, ti * C + ob * 512: ti * C + ob * 512 + 512],
                    pb[ti][0:128, 0:512]), "vec")
                p.fence("vector", "vec")
        p.wait("vector", "gdma")
        for ti in range(8):
            xa = att1[:, ti * C:(ti + 1) * C]
            p.run("vector", lambda e, xa=xa: e.reduce_sum(st1[:], xa, axis=AX.X), "vec")
            p.wait("scalar", "vec")
            p.run("scalar", lambda e, xa=xa: e.activation(jnk5[:], xa, AF.Square,
                                                          accum_out=st2[:]), "scl")
            p.wait("vector", "scl")
            p.fence("vector", "vec")
            p.run("vector", lambda e: e.tensor_single_scalar(st1[:], st1[:], 1.0 / C, ALU.mult), "vec")
            p.fence("vector", "vec")
            p.run("vector", lambda e: e.tensor_tensor(st3[:], st1[:], st1[:], ALU.mult), "vec")
            p.run("vector", lambda e: e.tensor_scalar(st2[:], st2[:], 1.0 / C, 1e-5, ALU.mult, ALU.add), "vec")
            p.fence("vector", "vec")
            p.run("vector", lambda e: e.tensor_sub(st2[:], st2[:], st3[:]), "vec")
            p.fence("vector", "vec")
            p.wait("scalar", "vec")
            p.run("scalar", lambda e: e.activation(st2[:], st2[:], AF.Sqrt), "scl")
            p.wait("vector", "scl")
            p.run("vector", lambda e: e.reciprocal(st2[:], st2[:]), "vec")
            p.fence("vector", "vec")
            p.run("vector", lambda e, xa=xa: e.tensor_scalar(xa, xa, st1[:], None, ALU.subtract), "vec")
            p.fence("vector", "vec")
            p.run("vector", lambda e, xa=xa: e.tensor_scalar(xa, xa, st2[:], None, ALU.mult), "vec")
            p.fence("vector", "vec")
            p.run("vector", lambda e, xa=xa: e.tensor_mul(xa, xa, gamb[:]), "vec")
            p.fence("vector", "vec")
            p.run("vector", lambda e, xa=xa: e.tensor_add(xa, xa, betb[:]), "vec")
            p.fence("vector", "vec")
            p.wait("sync", "vec")
            p.dma("sync", s_att1[ti * 128:(ti + 1) * 128, :], xa, "tdma")
        p.barrier()
        es_p5o.close()
        p.tap("att1", att1[:], (128, 8 * C), F32)

        es_sg = contextlib.ExitStack()
        ringf = WRing(p, "wrf", 2, 4096, dtype=F32, es=es_sg)
        sgstg = [es_sg.enter_context(nc.sbuf_tensor(f"sgstg{i}", [128, 4096], BF16, side="right"))
                 for i in range(2)]
        for _t in sgstg:
            p.claim(_t)
        for i in range(2):
            p.new_sem(f"sgd{i}")
        p.run("vector", lambda e: e.memset(acc1[:], 0.0), "vec")
        p.run("vector", lambda e: e.memset(acc2[:], 0.0), "vec")
        p.fence("vector", "vec")
        n_sg = [0]

        def sgn_gen(w_ap, n_cg, n_og, dst_scr, acc):
            for cg in range(n_cg):
                for og in range(n_og):
                    gi = ringf.n_issued
                    ringf.load(w_ap.rearrange("(cg p) o -> p cg o", p=128)
                               [:, cg * 8:(cg + 1) * 8, og * 512:(og + 1) * 512], cols=4096)
                    slot = n_sg[0] % 2
                    if n_sg[0] >= 2:
                        p.wait("scalar", f"sgd{slot}", 16 * (n_sg[0] // 2))
                    ringf.wait_granule("scalar", gi)
                    p.run("scalar", lambda e, slot=slot, gb=ringf.bufs[gi % ringf.nslots]: e.activation(
                        sgstg[slot][:], gb[:], AF.Sign), "scl")
                    ringf.wait_granule("vector", gi)
                    p.run("vector", lambda e, gb=ringf.bufs[gi % ringf.nslots]: e.reduce_sum(
                        st1[:], gb[:], axis=AX.X, apply_absolute_value=True), "vec")
                    p.fence("vector", "vec")
                    p.run("vector", lambda e, acc=acc: e.tensor_add(acc[:], acc[:], st1[:]), "vec")
                    p.fence("vector", "vec")
                    ringf.free_at.append([("scl", p.val("scl")), ("vec", p.val("vec"))])
                    p.wait("sync", "scl")
                    p.wait("sync", "vec")
                    p.dma("sync",
                          dst_scr.rearrange("(cg p) o -> p cg o", p=128)
                          [:, cg * 8:(cg + 1) * 8, og * 512:(og + 1) * 512],
                          sgstg[slot][:].rearrange("p (cg o) -> p cg o", cg=8),
                          f"sgd{slot}")
                    n_sg[0] += 1

        sgn_gen(f1w, 2, 16, s_sgn1, acc1)
        sgn_gen(f2w, 8, 4, s_sgn2, acc2)
        p.barrier()
        es_sg.close()
        p.run("vector", lambda e: e.tensor_copy(sw12[:, 0:1], acc1[:]), "vec")
        p.run("vector", lambda e: e.tensor_copy(sw12[:, 1:2], acc2[:]), "vec")
        p.run("vector", lambda e: e.memset(c_onesc[:], 1.0), "vec")
        p.run("vector", lambda e: e.memset(c_onesr[:], 1.0), "vec")
        p.fence("vector", "vec")
        p.wait("tensor", "vec")
        p.run("tensor", lambda e: e.matmul(pb[0][0:1, 0:2], c_onesc[:, 0:1],
                                           sw12[:, 0:2], start=True, stop=True), "ten")
        p.wait("scalar", "ten")
        p.run("scalar", lambda e: e.activation(sw12[0:1, 2:4], pb[0][0:1, 0:2],
                                               AF.Copy, scale=1.0 / (C * INNER)), "scl")
        p.wait("tensor", "scl")
        p.run("tensor", lambda e: e.matmul(pb[1][0:128, 0:2], c_onesr[0:1, :],
                                           sw12[0:1, 2:4], start=True, stop=True), "ten")
        p.wait("scalar", "ten")
        p.run("scalar", lambda e: e.activation(sw12[:, 0:2], pb[1][0:128, 0:2],
                                               AF.Copy), "scl")
        p.barrier()

        es_p6a = contextlib.ExitStack()
        xq1f = es_p6a.enter_context(nc.sbuf_tensor("xq1f", [128, NCC * 256], BF16, side="right"))
        p.claim(xq1f)
        hbuf = es_p6a.enter_context(nc.sbuf_tensor("hbuf", [128, 2 * INNER], F32, side="right"))
        p.claim(hbuf)
        h6i = es_p6a.enter_context(nc.sbuf_tensor("h6i", [128, C], I32, side="right"))
        p.claim(h6i)
        h6b = es_p6a.enter_context(nc.sbuf_tensor("h6b", [128, C], F32, side="right"))
        p.claim(h6b)
        h6bf = es_p6a.enter_context(nc.sbuf_tensor("h6bf", [128, INNER], BF16, side="right"))
        p.claim(h6bf)
        b1b = es_p6a.enter_context(nc.sbuf_tensor("b1b", [128, INNER], BF16, side="right"))
        p.claim(b1b)
        p.dma("gpsimd", b1b[:], f1b.broadcast_to([128, INNER]), "gdma")
        sfap = [es_p6a.enter_context(nc.sbuf_tensor(f"sfap{i}", [128, 1], F32, side="right"))
                for i in range(2)]
        for _t in sfap:
            p.claim(_t)
        for tp in range(4):
            p.wait("sync", "ten")
            for cc in range(NCC):
                p.dma_tr("sync", xq1f[:, cc * 256:(cc + 1) * 256],
                         s_xq1[tp * 256:(tp + 1) * 256, cc * 128:(cc + 1) * 128], "tdma")
            for tj in range(2):
                ti = tp * 2 + tj
                p.wait("sync", "vec")
                p.dma("sync", sfap[tj][:], s_qsc[LB + ti * 128: LB + (ti + 1) * 128, :], "sdma")
            p.wait("vector", "sdma")
            for tj in range(2):
                p.run("vector", lambda e, tj=tj: e.reciprocal(sfap[tj][:], sfap[tj][:]), "vec")
                p.fence("vector", "vec")
                p.run("vector", lambda e, tj=tj: e.tensor_scalar(
                    sfap[tj][:], sfap[tj][:], sw12[:, 0:1], None, ALU.mult), "vec")
                p.fence("vector", "vec")
            p.wait("tensor", "tdma")
            evac_vec = {}
            for ob in range(16):
                for ch in range(2):
                    if ch == 0 and ob >= 2:
                        p.wait("tensor", "vec", evac_vec[ob - 2])
                    gi = ring.n_issued
                    ring.load(s_sgn1.rearrange("(cg p) o -> p cg o", p=128)
                              [:, ch * 8:(ch + 1) * 8, ob * 512:(ob + 1) * 512], cols=4096)
                    ring.wait_granule("tensor", gi)
                    gbuf = ring.bufs[gi % ring.nslots]
                    for tj in range(2):
                        ti = tp * 2 + tj
                        psb = pb[tj * 2 + (ob % 2)]
                        if ch == 0 and ob < 2:
                            p.wait("tensor", "vec")
                            p.wait("tensor", "scl")

                        def mm61(e, ch=ch, tj=tj, ti=ti, psb=psb, gbuf=gbuf):
                            last = None
                            for c8 in range(8):
                                cc = ch * 8 + c8
                                last = e.matmul(psb[0:128, 0:512],
                                                xq1f[:, cc * 256 + tj * 128: cc * 256 + tj * 128 + 128],
                                                gbuf[:, c8 * 512:(c8 + 1) * 512],
                                                start=(cc == 0), stop=(cc == NCC - 1))
                            return last
                        p.run("tensor", mm61, "ten")
                    ring.done(1)
                p.wait("vector", "ten")
                for tj in range(2):
                    ti = tp * 2 + tj
                    psb = pb[tj * 2 + (ob % 2)]
                    p.run("vector", lambda e, tj=tj, ob=ob, psb=psb: e.tensor_scalar(
                        hbuf[:, tj * INNER + ob * 512: tj * INNER + ob * 512 + 512],
                        psb[0:128, 0:512], sfap[tj][:], None, ALU.mult), "vec")
                    p.fence("vector", "vec")
                evac_vec[ob] = p.val("vec")
            for tj in range(2):
                ti = tp * 2 + tj
                hb = hbuf[:, tj * INNER:(tj + 1) * INNER]
                p.wait("vector", "gdma")
                p.run("vector", lambda e, hb=hb: e.tensor_add(hb, hb, b1b[:]), "vec")
                p.fence("vector", "vec")
                p.wait("scalar", "vec")
                p.run("scalar", lambda e, hb=hb: e.activation(hb, hb, AF.Gelu), "scl")
                p.fence("scalar", "scl")
                p.wait("vector", "scl")
                p.wait("scalar", "tdma")
                p.run("scalar", lambda e, hb=hb: e.activation(h6bf[:], hb, AF.Square,
                                                              accum_out=st2[:]), "scl")
                p.run("vector", lambda e, hb=hb: e.reduce_max(st3[:], hb, axis=AX.X,
                                                              apply_absolute_value=True), "vec")
                p.wait("vector", "scl")
                p.fence("vector", "vec")
                p.run("vector", lambda e: e.tensor_scalar(st2[:], st2[:], 1.0 / INNER, 1e-6, ALU.mult, ALU.add), "vec")
                p.fence("vector", "vec")
                p.wait("scalar", "vec")
                p.run("scalar", lambda e: e.activation(st2[:], st2[:], AF.Sqrt), "scl")
                p.wait("vector", "scl")
                p.run("vector", lambda e: e.reciprocal(st2[:], st2[:]), "vec")
                p.fence("vector", "vec")
                p.run("vector", lambda e: e.tensor_single_scalar(st2[:], st2[:], INNER ** -0.5, ALU.mult), "vec")
                p.fence("vector", "vec")
                p.run("vector", lambda e: e.tensor_tensor(st3[:], st3[:], st2[:], ALU.mult), "vec")
                p.fence("vector", "vec")
                p.run("vector", lambda e: e.tensor_scalar_max(st3[:], st3[:], 1e-5), "vec")
                p.fence("vector", "vec")
                p.run("vector", lambda e: e.reciprocal(st3[:], st3[:]), "vec")
                p.fence("vector", "vec")
                p.run("vector", lambda e: e.tensor_single_scalar(st3[:], st3[:], 127.0, ALU.mult), "vec")
                p.fence("vector", "vec")
                p.run("vector", lambda e, ti=ti: e.tensor_copy(qsc2[:, ti:ti + 1], st3[:]), "vec")
                p.run("vector", lambda e: e.tensor_tensor(st3[:], st3[:], st2[:], ALU.mult), "vec")
                p.fence("vector", "vec")
                p.run("vector", lambda e, hb=hb: e.tensor_scalar(hb, hb, st3[:], None, ALU.mult), "vec")
                p.fence("vector", "vec")
                for ck in range(INNER // C):
                    hc_ = hb[:, ck * C:(ck + 1) * C]
                    p.run("vector", lambda e, hc_=hc_: e.tensor_copy(h6i[:], hc_), "vec")
                    p.fence("vector", "vec")
                    p.run("vector", lambda e: e.tensor_copy(h6b[:], h6i[:]), "vec")
                    p.fence("vector", "vec")
                    p.run("vector", lambda e, hc_=hc_: e.tensor_sub(hc_, hc_, h6b[:]), "vec")
                    p.fence("vector", "vec")
                    p.run("vector", lambda e, hc_=hc_: e.scalar_tensor_tensor(h6b[:], hc_, 0.5, h6b[:], ALU.is_ge, ALU.add), "vec")
                    p.fence("vector", "vec")
                    p.run("vector", lambda e, hc_=hc_: e.tensor_single_scalar(hc_, hc_, -0.5, ALU.is_le), "vec")
                    p.fence("vector", "vec")
                    p.run("vector", lambda e, hc_=hc_, ck=ck: e.tensor_tensor(
                        h6bf[:, ck * C:(ck + 1) * C], h6b[:], hc_, ALU.subtract), "vec")
                    p.fence("vector", "vec")
                p.wait("sync", "vec")
                p.dma("sync", s_xq2[ti * 128:(ti + 1) * 128, :], h6bf[:], "tdma")
        p.barrier()
        es_p6a.close()

        es_p6b = contextlib.ExitStack()
        xq2f = es_p6b.enter_context(nc.sbuf_tensor("xq2f", [128, 64 * 256], BF16, side="right"))
        p.claim(xq2f)
        b2b = es_p6b.enter_context(nc.sbuf_tensor("b2b", [128, C], BF16, side="right"))
        p.claim(b2b)
        xt1 = es_p6b.enter_context(nc.sbuf_tensor("xt1", [128, C], F32, side="right"))
        p.claim(xt1)
        at1 = es_p6b.enter_context(nc.sbuf_tensor("at1", [128, C], F32, side="right"))
        p.claim(at1)
        of32 = es_p6b.enter_context(nc.sbuf_tensor("of32", [128, C], F32, side="right"))
        p.claim(of32)
        p.dma("gpsimd", b2b[:], f2b.broadcast_to([128, C]), "gdma")
        for tp in range(4):
            p.wait("sync", "ten")
            for ic in range(64):
                p.dma_tr("sync", xq2f[:, ic * 256:(ic + 1) * 256],
                         s_xq2[tp * 256:(tp + 1) * 256, ic * 128:(ic + 1) * 128], "tdma")
            p.wait("tensor", "tdma")
            for icg in range(8):
                for ob in range(4):
                    gi = ring.n_issued
                    ring.load(s_sgn2.rearrange("(cg p) o -> p cg o", p=128)
                              [:, icg * 8:(icg + 1) * 8, ob * 512:(ob + 1) * 512], cols=4096)
                    ring.wait_granule("tensor", gi)
                    gbuf = ring.bufs[gi % ring.nslots]
                    if icg == 0 and ob == 0:
                        p.wait("tensor", "vec")

                    def mm62(e, icg=icg, ob=ob, gbuf=gbuf):
                        last = None
                        for tj in range(2):
                            for i8 in range(8):
                                ic = icg * 8 + i8
                                last = e.matmul(pb[tj * 4 + ob][0:128, 0:512],
                                                xq2f[:, ic * 256 + tj * 128: ic * 256 + tj * 128 + 128],
                                                gbuf[:, i8 * 512:(i8 + 1) * 512],
                                                start=(ic == 0), stop=(ic == 63))
                        return last
                    p.run("tensor", mm62, "ten")
                    ring.done(1)
            for tj in range(2):
                ti = tp * 2 + tj
                p.wait("sync", "vec")
                p.dma("sync", xt1[:], s_x1[LB + ti * 128: LB + (ti + 1) * 128, :], "sdma")
                p.dma("sync", at1[:], s_att1[ti * 128:(ti + 1) * 128, :], "sdma")
                p.wait("vector", "ten")
                p.wait("vector", "sdma")
                p.wait("vector", "gdma")
                p.run("vector", lambda e, ti=ti: e.tensor_copy(st3[:], qsc2[:, ti:ti + 1]), "vec")
                p.fence("vector", "vec")
                p.run("vector", lambda e: e.reciprocal(st3[:], st3[:]), "vec")
                p.fence("vector", "vec")
                p.run("vector", lambda e: e.tensor_scalar(st3[:], st3[:], sw12[:, 1:2], None, ALU.mult), "vec")
                p.fence("vector", "vec")
                for ob in range(4):
                    p.run("vector", lambda e, tj=tj, ob=ob: e.tensor_scalar(
                        of32[:, ob * 512:(ob + 1) * 512], pb[tj * 4 + ob][0:128, 0:512],
                        st3[:], None, ALU.mult), "vec")
                    p.fence("vector", "vec")
                p.run("vector", lambda e: e.tensor_add(of32[:], of32[:], b2b[:]), "vec")
                p.fence("vector", "vec")
                p.run("vector", lambda e: e.tensor_add(of32[:], of32[:], at1[:]), "vec")
                p.fence("vector", "vec")
                p.run("vector", lambda e: e.tensor_add(of32[:], of32[:], xt1[:]), "vec")
                p.fence("vector", "vec")
                p.wait("sync", "vec")
                p.dma("sync", out[ti * 128:(ti + 1) * 128, :], of32[:], "sdma")
        p.barrier()
        es_p6b.close()
        if "scores" in DEBUG_TAPS:
            p.tap("scores", sc_dbg[:], (128, 9 * E), F32)
        if "logits" in DEBUG_TAPS:
            p.tap("logits", lg_dbg[:], (128, 9 * E), F32)
        p.tap("cw", cw_t[:], (128, len(QTS) * E), F32)
        p.tap("xne", xne_fm[:], (128, NCC * EXT), BF16)
        p.tap("xnf", xnf_fm[:], (128, NCC * T), BF16)

        p.wait("sync", "sdma")
        p.wait("sync", "tdma")
        p.wait("sync", "gdma")

        p.finish()
    return nc


def kernel(**inputs):
    shared = host_prep(inputs)
    x = np.asarray(inputs["L0_attn_x"], np.float32)
    nc = build_program()
    in_maps = []
    for c in range(8):
        m = dict(shared)
        m.update(percore_prep(x, c))
        in_maps.append(m)
    res = run_bass_kernel_spmd(nc, in_maps, core_ids=list(range(8)))
    out = np.zeros((4, T, C), np.float32)
    for c in range(8):
        b, heavy = c // 2, (c % 2 == 0)
        rows = slice(1024, 2048) if heavy else slice(0, 1024)
        out[b, rows] = res.results[c]["out"]
    return out
